# version 10
# baseline (speedup 1.0000x reference)
import numpy as np
import ml_dtypes
import concourse.bass as bass
import concourse.mybir as mybir
from concourse.bass_utils import run_bass_kernel_spmd

F32 = mybir.dt.float32
BF16 = mybir.dt.bfloat16
AF = mybir.ActivationFunctionType
ALU = mybir.AluOpType
AX = mybir.AxisListType
BF16NP = ml_dtypes.bfloat16

E = 4096
NQH = 32
NKVH = 8
D = 128
NEC = 32
NFC = 48
ROPE_BASE = 10000.0
EPS = 1e-5
NEGINF = -1.0e30



def CP(e, d, s):
    if isinstance(e, bass.BassScalarEngine):
        return e.copy(d, s)
    return e.tensor_copy(d, s)


def _frow(fc):
    if fc < 32:
        return (fc // 8) * 1024 + (fc % 8) * 128
    if fc < 40:
        return 4096 + (fc - 32) * 128
    return 5120 + (fc - 40) * 128


class KB:
    ENGS = ("sync", "scalar", "vector", "tensor", "gpsimd")

    def __init__(self, nc):
        self.nc = nc
        self.ops = {k: [] for k in self.ENGS}
        self.sem = {}

    def S(self, name):
        if name not in self.sem:
            self.sem[name] = [self.nc.alloc_semaphore(name), 0]
        return self.sem[name]

    def cnt(self, name):
        return self.S(name)[1]

    def do(self, eng, make, inc=None, n=1, waits=()):
        tgt = None
        if inc is not None:
            s = self.S(inc)
            s[1] += n
            tgt = (s[0], n)
        pairs = tuple((self.S(nm)[0], v) for (nm, v) in waits if v)

        def f(e, make=make, tgt=tgt, pairs=pairs):
            for (h, v) in pairs:
                e.wait_ge(h, v)
            i = make(e)
            if tgt is not None:
                i.then_inc(*tgt)

        self.ops[eng].append(f)
        return self.sem[inc][1] if inc is not None else None

    def wait(self, eng, *waits):
        pairs = tuple((self.S(nm)[0], v) for (nm, v) in waits if v)
        self.ops[eng].append(lambda e, pairs=pairs: [e.wait_ge(h, v) for (h, v) in pairs])

    def barrier(self):
        snap = tuple((s[0], s[1]) for s in self.sem.values() if s[1] > 0)
        for eng in self.ENGS:
            self.ops[eng].append(
                lambda e, snap=snap: [e.wait_ge(h, v) for (h, v) in snap])


def build_program(NT=16):
    S = NT * 128
    TBS = NT // 4
    NKC = S // 128
    nc = bass.Bass("TRN2", target_bir_lowering=False, debug=False)

    xin = nc.dram_tensor("xin", [S, E], F32, kind="ExternalInput").ap()
    wqkv = nc.dram_tensor("wqkv", [NFC, 128, NEC * 128], F32, kind="ExternalInput").ap()
    wout = nc.dram_tensor("wout", [8, 4, 128, 8 * 512], F32, kind="ExternalInput").ap()
    wffn = nc.dram_tensor("wffn", [8, 4, 128, 8 * 512], F32, kind="ExternalInput").ap()
    gate_st = nc.dram_tensor("gate_st", [128, 256], F32, kind="ExternalInput").ap()
    ics_in = nc.dram_tensor("ics_in", [8, E], F32, kind="ExternalInput").ap()
    ocs_in = nc.dram_tensor("ocs_in", [8, E], F32, kind="ExternalInput").ap()
    bias_in = nc.dram_tensor("bias_in", [1, E], F32, kind="ExternalInput").ap()
    ident_in = nc.dram_tensor("ident_in", [128, 128], BF16, kind="ExternalInput").ap()
    rot_in = nc.dram_tensor("rot_in", [128, 128], BF16, kind="ExternalInput").ap()
    mask_in = nc.dram_tensor("mask_in", [128, 128], BF16, kind="ExternalInput").ap()
    maskT_in = nc.dram_tensor("maskT_in", [128, 128], BF16, kind="ExternalInput").ap()
    cq_in = nc.dram_tensor("cq_in", [128, S], F32, kind="ExternalInput").ap()
    sq_in = nc.dram_tensor("sq_in", [128, S], F32, kind="ExternalInput").ap()
    ck_in = nc.dram_tensor("ck_in", [128, S], F32, kind="ExternalInput").ap()
    sk_in = nc.dram_tensor("sk_in", [128, S], F32, kind="ExternalInput").ap()
    out = nc.dram_tensor("out", [S, E], F32, kind="ExternalOutput").ap()

    Qs = nc.dram_tensor("Qs", [128, TBS * 16384], BF16, kind="Internal").ap()
    Ks = nc.dram_tensor("Ks", [128, NKVH * S], BF16, kind="Internal").ap()
    Vs = nc.dram_tensor("Vs", [128, NKVH * S], BF16, kind="Internal").ap()
    As = nc.dram_tensor("As", [128, NQH * S], BF16, kind="Internal").ap()
    out2s = nc.dram_tensor("out2s", [S, E], BF16, kind="Internal").ap()
    x2s = nc.dram_tensor("x2s", [S, E], F32, kind="Internal").ap()
    ys = nc.dram_tensor("ys", [S, E], BF16, kind="Internal").ap()

    BASE = 16896
    A0 = BASE
    S0 = BASE + 131072
    PP = BASE + 131072 + 57344
    al = nc.alloc_sbuf_tensor_at

    arena = al("arena", [128, NEC * S], BF16, offset=A0)
    k_sb = al("k_sb", [128, NKVH * S], BF16, offset=A0)
    vtm = al("vtm", [128, NKVH * S], BF16, offset=A0 + NKVH * S * 2)
    q_sb = [al(f"q_sb{i}", [128, 16384], BF16,
               offset=A0 + 4 * NKVH * S + i * 32768) for i in range(2)]
    bias_bc = al("bias_bc", [128, E], F32, offset=A0)
    bias_sb = al("bias_sb", [1, E], F32, offset=A0 + E * 4)
    y7b = [al(f"y7b{i}", [128, E], BF16, offset=A0 + 32768 + i * 8192) for i in range(2)]
    x27b = [al(f"x27b{i}", [128, E], F32, offset=A0 + 49152 + i * 16384) for i in range(2)]
    y1b = [al(f"y1b{i}", [128, E], F32, offset=A0 + 81920 + i * 16384) for i in range(2)]

    x_t = [al(f"x_t{i}", [128, E], F32, offset=S0 + i * 16384) for i in range(2)]
    xln_t = [al(f"xln_t{i}", [128, E], BF16, offset=S0 + 32768 + i * 8192) for i in range(2)]
    dump = al("dump", [128, E], BF16, offset=S0 + 49152)
    wq_sb = [al(f"wq_sb{i}", [128, 4096], BF16, offset=S0 + i * 8192) for i in range(2)]
    T2 = S0 + 16384
    cq_sb = al("cq_sb", [128, S], BF16, offset=T2)
    sq_sb = al("sq_sb", [128, S], BF16, offset=T2 + 2 * S)
    ck_sb = al("ck_sb", [128, S], BF16, offset=T2 + 4 * S)
    sk_sb = al("sk_sb", [128, S], BF16, offset=T2 + 6 * S)
    R2 = T2 + 8 * S
    qb_sb = [al(f"qb_sb{i}", [128, 512], BF16, offset=R2 + i * 1024) for i in range(4)]
    scr1 = [al(f"scr1_{i}", [128, 512], F32, offset=R2 + 4096 + i * 2048) for i in range(4)]
    scr2 = [al(f"scr2_{i}", [128, 512], F32, offset=R2 + 12288 + i * 2048) for i in range(4)]
    qout_sb = [al(f"qout_sb{i}", [128, 512], BF16, offset=R2 + 20480 + i * 1024)
               for i in range(4)]
    vraw = al("vraw", [128, NKVH * S], BF16, offset=A0 + 4 * NKVH * S + 32768)
    ptb = [al(f"ptb{i}", [128, 512], BF16, offset=S0 + i * 1024) for i in range(6)]
    astg = [al(f"astg{i}", [128, 512], BF16, offset=S0 + 6144 + i * 1024) for i in range(2)]
    bcast_sb = [al(f"bcast_sb{i}", [128, 512], F32, offset=S0 + 8192 + i * 2048)
                for i in range(2)]
    nmxT_sb = [al(f"nmxT_sb{i}", [1, 512], BF16, offset=S0 + 12288 + i * 1024)
               for i in range(4)]
    recT_sb = [al(f"recT_sb{i}", [1, 512], F32, offset=S0 + 16384 + i * 2048)
               for i in range(2)]
    knsq = al("knsq", [128, S], BF16, offset=S0 + 20480)
    qsq = [al(f"qsq{i}", [128, 512], BF16, offset=S0 + 20480 + 2 * S + i * 1024)
           for i in range(2)]
    nrow_f = al("nrow_f", [1, 512], F32, offset=S0 + 20480 + 2 * S + 2048)
    wslot = [al(f"wslot{i}", [128, 4096], BF16, offset=S0 + i * 8192) for i in range(5)]
    os_sb = [al(f"os_sb{i}", [128, 512], BF16, offset=S0 + 40960 + i * 1024) for i in range(2)]
    ystg = [al(f"ystg{i}", [128, 512], BF16, offset=S0 + 43008 + i * 1024) for i in range(2)]
    o2stg = [al(f"o2stg{i}", [128, 512], BF16, offset=S0 + 45056 + i * 1024) for i in range(2)]
    o2_t = al("o2_t", [128, E], BF16, offset=S0)
    x5_t = al("x5_t", [128, E], F32, offset=S0 + 8192)
    ln2tmp = al("ln2tmp", [128, E], F32, offset=S0 + 24576)
    xln3 = al("xln3", [128, E], BF16, offset=S0 + 40960)
    y7_t = al("y7_t", [128, E], BF16, offset=S0)
    x27_t = al("x27_t", [128, E], F32, offset=S0 + 8192)
    y1_t = al("y1_t", [128, E], F32, offset=S0 + 24576)
    ident = al("ident", [128, 128], BF16, offset=PP)
    rot = al("rot", [128, 128], BF16, offset=PP + 256)
    ones_f = al("ones_f", [1, 128], F32, offset=PP + 512)
    ones_bf = al("ones_bf", [1, 128], BF16, offset=PP + 20480)
    ones_bf1 = al("ones_bf1", [128, 1], BF16, offset=PP + 20736)
    maskT_sb = al("maskT_sb", [128, 128], BF16, offset=PP + 20800)
    mask_sb = al("mask_sb", [128, 128], BF16, offset=PP + 1024)
    gate_sb = al("gate_sb", [128, 256], BF16, offset=PP + 1536)
    ics_sb = al("ics_sb", [8, E], BF16, offset=PP + 2048)
    ocs_sb = al("ocs_sb", [8, E], BF16, offset=PP + 10240)
    routT = al("routT", [8, S], BF16, offset=PP + 18432)
    stats = al("stats", [128, 32], F32, offset=PP + 18432 + 2 * S)
    routf = al("routf", [128, 8], F32, offset=PP + 18432 + 2 * S + 128)
    routb = al("routb", [128, 8], BF16, offset=PP + 18432 + 2 * S + 160)
    consts = al("consts", [128, 4], F32, offset=PP + 18432 + 2 * S + 192)

    PA = nc.alloc_psum_tensor("PA", [128, 1024], F32)
    PB = nc.alloc_psum_tensor("PB", [128, 1024], F32)
    PC = nc.alloc_psum_tensor("PC", [128, 512], F32)
    PD = nc.alloc_psum_tensor("PD", [128, 512], F32)
    PE2 = nc.alloc_psum_tensor("PE2", [128, 512], F32)
    PF = nc.alloc_psum_tensor("PF", [128, 512], F32)
    SLOTS = [
        dict(ap=PA[:, 0:512], eng="vector", pe_done=0, ev_done=0),
        dict(ap=PA[:, 512:1024], eng="scalar", pe_done=0, ev_done=0),
        dict(ap=PB[:, 0:512], eng="vector", pe_done=0, ev_done=0),
        dict(ap=PB[:, 512:1024], eng="scalar", pe_done=0, ev_done=0),
        dict(ap=PC[:, :], eng="vector", pe_done=0, ev_done=0),
        dict(ap=PD[:, :], eng="scalar", pe_done=0, ev_done=0),
    ]

    kb = KB(nc)
    do, wait, cnt = kb.do, kb.wait, kb.cnt
    ES = {"vector": "sV", "scalar": "sA", "tensor": "sT", "gpsimd": "sP"}
    slot_i = [0]

    def next_slot():
        sl = SLOTS[slot_i[0] % 6]
        slot_i[0] += 1
        return sl

    def evac(sl, dst_ap, extra=()):
        eng = sl["eng"]
        c = do(eng, lambda e, a=sl["ap"], d=dst_ap: CP(e, d, a[:, :]),
               inc=ES[eng], waits=(("sT", sl["pe_done"]),) + tuple(extra))
        sl["ev_done"] = (ES[eng], c)
        return c

    def arena_dst(g, t):
        return arena.ap().rearrange("p (a b) -> p a b", b=S)[
            :, g * 4:g * 4 + 4, t * 128:(t + 1) * 128]

    def ln_stats(src_ap, c, pre_waits_v, pre_waits_a=(), dmp=None):
        if dmp is None:
            dmp = dump[:, :]
        vsum = do("vector", lambda e: e.reduce_sum(stats[:, c:c + 1], src_ap, axis=AX.X),
                  inc="sV", waits=pre_waits_v)
        vneg = do("vector", lambda e: e.tensor_scalar_mul(
            stats[:, c:c + 1], stats[:, c:c + 1], -1.0 / E), inc="sV", waits=(("sV", vsum),))
        asq = do("scalar", lambda e, dmp=dmp: e.activation(
            dmp, src_ap, AF.Square, bias=stats[:, c:c + 1], scale=1.0,
            accum_out=stats[:, c + 1:c + 2]), inc="sA",
            waits=(("sV", vneg),) + tuple(pre_waits_a))
        am = do("scalar", lambda e: e.mul(stats[:, c + 1:c + 2], stats[:, c + 1:c + 2], 1.0 / E),
                inc="sA", waits=(("sA", asq),))
        al_ = do("scalar", lambda e: e.activation(
            stats[:, c + 2:c + 3], stats[:, c + 1:c + 2], AF.Ln,
            bias=consts[:, 0:1], scale=1.0), inc="sA", waits=(("sA", am),))
        am2 = do("scalar", lambda e: e.mul(stats[:, c + 2:c + 3], stats[:, c + 2:c + 3], -0.5),
                 inc="sA", waits=(("sA", al_),))
        ar = do("scalar", lambda e: e.activation(
            stats[:, c + 3:c + 4], stats[:, c + 2:c + 3], AF.Exp, bias=0.0, scale=1.0),
            inc="sA", waits=(("sA", am2),))
        return stats[:, c:c + 1], stats[:, c + 3:c + 4], ar

    for dst, src in ((ident[:, :], ident_in[:, :]), (rot[:, :], rot_in[:, :]),
                     (mask_sb[:, :], mask_in[:, :]), (maskT_sb[:, :], maskT_in[:, :])):
        do("sync", lambda e, d=dst, s=src: e.dma_start(out=d, in_=s), inc="sC", n=16)
    for dst, src in ((gate_sb[:, :], gate_st[:, :]), (ics_sb[:, :], ics_in[:, :]),
                     (ocs_sb[:, :], ocs_in[:, :])):
        do("gpsimd", lambda e, d=dst, s=src: e.dma_start(out=d, in_=s), inc="sCg", n=16)
    CTOT = cnt("sC")
    CGTOT = cnt("sCg")
    do("vector", lambda e: e.memset(consts[:, 0:1], EPS), inc="sV")
    do("vector", lambda e: e.memset(ones_f[:, :], 1.0), inc="sV")
    do("vector", lambda e: e.memset(ones_bf[:, :], 1.0), inc="sV")
    do("vector", lambda e: e.memset(ones_bf1[:, :], 1.0), inc="sV")
    CONSTV = cnt("sV")

    xload = [0, 0]
    pe_done_tile = {}
    for t in range(NT):
        p = t % 2
        w = (("sT", pe_done_tile[t - 2]),) if t >= 2 else ()
        do("scalar", lambda e, p=p, t=t: e.dma_start(
            out=x_t[p][:, :], in_=xin[t * 128:(t + 1) * 128, :]),
            inc="sX%d" % p, n=16, waits=w)
        xload[p] += 16
        c0 = 0 if p == 0 else 4
        negm, rstd, ar = ln_stats(x_t[p][:, :], c0,
                                  (("sX%d" % p, xload[p]), ("sV", CONSTV)))
        wxln = (("sA", ar),)
        if t >= 2:
            wxln = wxln + (("sT", pe_done_tile[t - 2]),)
        vts = do("vector", lambda e, p=p, negm=negm, rstd=rstd: e.tensor_scalar(
            xln_t[p][:, :], x_t[p][:, :], negm, rstd, op0=ALU.add, op1=ALU.mult),
            inc="sV", waits=wxln)
        for g in range(8):
            sl = next_slot()
            for j in range(4):
                fc = g * 4 + j
                wts = ()
                if j == 0:
                    wts = (("sV", vts), ("sC", CTOT))
                    if sl["ev_done"]:
                        wts = wts + (sl["ev_done"],)
                do("tensor", lambda e, p=p, fc=fc, j=j, a=sl["ap"]: e.matmul(
                    a[:, j * 128:(j + 1) * 128], xln_t[p][:, fc * 128:(fc + 1) * 128],
                    ident[:, :], start=True, stop=True),
                   inc="sT" if j == 3 else None, waits=wts)
            sl["pe_done"] = cnt("sT")
            evac(sl, arena_dst(g, t))
        pe_done_tile[t] = cnt("sT")
    kb.barrier()

    for dst, src in ((cq_sb[:, :], cq_in[:, :]), (sq_sb[:, :], sq_in[:, :]),
                     (ck_sb[:, :], ck_in[:, :]), (sk_sb[:, :], sk_in[:, :])):
        do("gpsimd", lambda e, d=dst, s=src: e.dma_start(out=d, in_=s), inc="sC2", n=16)
    C2TOT = cnt("sC2")

    wload = [0, 0]
    qkspill = [0, 0, 0, 0]
    pe_fc_done = {}
    swap_free = [0, 0]
    swap_bank = [PE2, PF]
    rope = {}
    pend = []
    chunk = 0

    def flush_rope():
        while pend:
            k = pend.pop(0)
            r = rope[k]
            cp2 = k % 4
            sp = k % 2
            wts = (("sT", r["slot_pe"]), ("sA", r["aev"]))
            if swap_free[sp]:
                wts = wts + (("sV", swap_free[sp]),)
            do("tensor", lambda e, k=k, b=swap_bank[sp]: e.matmul(
                b[:, :], rot[:, :], qb_sb[k % 4][:, :], start=True, stop=True),
               inc="sT", waits=wts)
            r["swap"] = cnt("sT")
            wts = (("sT", r["swap"]), ("sP", r["t1"]), ("sC2", C2TOT))
            if qkspill[cp2]:
                wts = wts + (("sQK%d" % cp2, qkspill[cp2]),)
            v1 = do("vector", lambda e, k=k, b=swap_bank[sp], st=r["stab"], tb=r["tb"]:
                    e.tensor_mul(scr2[k % 4][:, :], b[:, :],
                                 st[:, tb * 512:(tb + 1) * 512]),
                    inc="sV", waits=wts)
            swap_free[sp] = v1
            r["v1"] = v1
            v2 = do("vector", lambda e, k=k: e.tensor_add(
                qout_sb[k % 4][:, :], scr1[k % 4][:, :], scr2[k % 4][:, :]),
                inc="sV", waits=(("sV", v1),))
            r["v2"] = v2
            do("sync", lambda e, d=r["dst"], k=k: e.dma_start(out=d, in_=qout_sb[k % 4][:, :]),
               inc="sQK%d" % cp2, n=16, waits=(("sV", v2),))
            qkspill[cp2] += 16

    for fc in range(NFC):
        p = fc % 2
        wts = (("sT", pe_fc_done[fc - 2]),) if fc >= 2 else ()
        do("gpsimd", lambda e, p=p, fc=fc: e.dma_start(out=wq_sb[p][:, :], in_=wqkv[fc]),
           inc="sW%d" % p, n=16, waits=wts)
        wload[p] += 16
        is_v = fc >= 40
        is_q = fc < 32
        for tb in range(TBS):
            sl = next_slot()
            for ec in range(NEC):
                wts2 = ()
                if ec == 0:
                    wts2 = (("sW%d" % p, wload[p]),)
                    if sl["ev_done"]:
                        wts2 = wts2 + (sl["ev_done"],)
                do("tensor", lambda e, p=p, ec=ec, tb=tb, a=sl["ap"]: e.matmul(
                    a[:, :], wq_sb[p][:, ec * 128:(ec + 1) * 128],
                    arena[:, ec * S + tb * 512: ec * S + (tb + 1) * 512],
                    start=(ec == 0), stop=(ec == NEC - 1)),
                   inc="sT" if ec == NEC - 1 else None, waits=wts2)
            sl["pe_done"] = cnt("sT")
            if is_v:
                cp2 = chunk % 4
                h = fc - 40
                wev = (("sT", sl["pe_done"]),)
                if qkspill[cp2]:
                    wev = wev + (("sQK%d" % cp2, qkspill[cp2]),)
                ev = do(sl["eng"], lambda e, a=sl["ap"], k=chunk: CP(
                    e, qout_sb[k % 4][:, :], a[:, :]), inc=ES[sl["eng"]], waits=wev)
                sl["ev_done"] = (ES[sl["eng"]], ev)
                do("sync", lambda e, h=h, tb=tb, k=chunk: e.dma_start(
                    out=Vs[:, h * S + tb * 512: h * S + (tb + 1) * 512],
                    in_=qout_sb[k % 4][:, :]),
                   inc="sQK%d" % cp2, n=16, waits=((ES[sl["eng"]], ev),))
                qkspill[cp2] += 16
            else:
                wev = (("sT", sl["pe_done"]),)
                if chunk - 4 in rope:
                    r4 = rope[chunk - 4]
                    wev = wev + (("sT", r4["swap"]), ("sP", r4["t1"]))
                aev = do("scalar", lambda e, a=sl["ap"], k=chunk: CP(
                    e, qb_sb[k % 4][:, :], a[:, :]), inc="sA", waits=wev)
                sl["ev_done"] = ("sA", aev)
                ctab = cq_sb if is_q else ck_sb
                stab = sq_sb if is_q else sk_sb
                wp = (("sA", aev), ("sC2", C2TOT))
                if chunk - 4 in rope:
                    wp = wp + (("sV", rope[chunk - 4]["v2"]),)
                t1 = do("gpsimd", lambda e, k=chunk, ct=ctab, tb=tb: e.tensor_mul(
                    scr1[k % 4][:, :], qb_sb[k % 4][:, :],
                    ct[:, tb * 512:(tb + 1) * 512]), inc="sP", waits=wp)
                if is_q:
                    dst = Qs[:, tb * 16384 + fc * 512: tb * 16384 + (fc + 1) * 512]
                else:
                    dst = Ks[:, (fc - 32) * S + tb * 512: (fc - 32) * S + (tb + 1) * 512]
                rope[chunk] = dict(slot_pe=sl["pe_done"], aev=aev, t1=t1, tb=tb,
                                   stab=stab, dst=dst, swap=0, v1=0, v2=0)
                pend.append(chunk)
            chunk += 1
        flush_rope()
        pe_fc_done[fc] = cnt("sT")
    flush_rope()
    wait("sync", ("sQK0", qkspill[0]), ("sQK1", qkspill[1]),
         ("sQK2", qkspill[2]), ("sQK3", qkspill[3]))
    kb.barrier()

    do("scalar", lambda e: e.dma_start(out=k_sb[:, :], in_=Ks[:, :]), inc="sKV", n=16)
    do("scalar", lambda e: e.dma_start(out=vraw[:, :], in_=Vs[:, :]), inc="sKV", n=16)
    KVTOT = cnt("sKV")
    for h in range(NKVH):
        for kcg in range(NKC // 4):
            sl = next_slot()
            for j in range(4):
                kc = kcg * 4 + j
                wts = ()
                if j == 0:
                    wts = (("sKV", KVTOT),)
                    if sl["ev_done"]:
                        wts = wts + (sl["ev_done"],)
                do("tensor", lambda e, h=h, kc=kc, j=j, a=sl["ap"]: e.matmul(
                    a[:, j * 128:(j + 1) * 128],
                    vraw[:, h * S + kc * 128: h * S + (kc + 1) * 128],
                    ident[:, :], start=True, stop=True),
                   inc="sT" if j == 3 else None, waits=wts)
            sl["pe_done"] = cnt("sT")
            dst = vtm.ap().rearrange("p (a b) -> p a b", b=1024)[
                :, kcg * 4:kcg * 4 + 4, h * 128:(h + 1) * 128]
            evac(sl, dst)
    VT_T = cnt("sT")
    VT_A, VT_V = cnt("sA"), cnt("sV")

    ALPHA = float(1.0 / np.sqrt(np.float64(D)))
    kn_pe = VT_T
    for H in range(NKVH):
        asq = do("scalar", lambda e, H=H: e.activation(
            knsq[:, :], k_sb[:, H * S:(H + 1) * S], AF.Square, bias=0.0, scale=1.0),
            inc="sA", waits=(("sKV", KVTOT), ("sT", kn_pe)))
        vprev = 0
        for kcb in range(S // 512):
            do("tensor", lambda e, kcb=kcb: e.matmul(
                PA[0:1, 0:512], ones_bf1[:, 0:1], knsq[:, kcb * 512:(kcb + 1) * 512],
                start=True, stop=True),
               inc="sT", waits=(("sA", asq), ("sV", CONSTV), ("sV", VT_V),
                                ("sA", VT_A),
                                ("sV", vprev) if vprev else ("sV", 0)))
            pe_n = cnt("sT")
            vprev = do("vector", lambda e, kcb=kcb, H=H: e.reduce_max(
                stats[0:1, 24:25] if kcb == 0 else stats[0:1, 25:26],
                PA[0:1, 0:512], axis=AX.X), inc="sV", waits=(("sT", pe_n),))
            if kcb > 0:
                vprev = do("vector", lambda e: e.tensor_tensor(
                    stats[0:1, 24:25], stats[0:1, 24:25], stats[0:1, 25:26],
                    op=ALU.max), inc="sV", waits=(("sV", vprev),))
        kn_pe = cnt("sT")
        va = do("vector", lambda e: e.tensor_scalar_mul(
            stats[0:1, 24:25], stats[0:1, 24:25], ALPHA * ALPHA),
            inc="sV", waits=(("sV", vprev),))
        al_ = do("scalar", lambda e, H=H: e.activation(
            stats[0:1, 16 + H:17 + H], stats[0:1, 24:25], AF.Ln, bias=0.0, scale=1.0),
            inc="sA", waits=(("sV", va),))
        do("scalar", lambda e, H=H: e.mul(
            stats[0:1, 16 + H:17 + H], stats[0:1, 16 + H:17 + H], 0.5),
            inc="sA", waits=(("sA", al_),))
    KN_A = cnt("sA")

    stbank = [dict(ap=PB[:, 0:512], exp=0), dict(ap=PB[:, 512:1024], exp=0),
              dict(ap=PC[:, :], exp=0)]
    sums_banks = [PD[:, :], PA[:, 512:1024]]
    NPB = 6
    pbuf_user = [0] * 6
    nmxT_used = [0, 0, 0, 0]
    nmx_ready = [0, 0, 0, 0]
    qn_pe = [0, 0]
    qn_chain = [0]
    bcast_used = [0, 0]
    pvbank = [dict(ap=PE2, free=0), dict(ap=PF, free=0)]
    bc_last = [0, 0]
    qload = [0, 0]
    qhist = {}
    aspill = [0, 0]
    pe_tb_done = {}

    do("scalar", lambda e: e.dma_start(out=q_sb[0][:, :], in_=Qs[:, 0:16384]),
       inc="sQL0", n=16)
    qload[0] += 16

    NG = TBS * NQH

    def mk_nmx(g):
        tb_, hg_ = divmod(g, NQH)
        H = hg_ // 4
        qp_ = tb_ % 2
        par = g % 2
        asq = do("scalar", lambda e, hg_=hg_, qp_=qp_, par=par: e.activation(
            qsq[par][:, :], q_sb[qp_][:, hg_ * 512:(hg_ + 1) * 512], AF.Square,
            bias=0.0, scale=1.0),
            inc="sA", waits=(("sQL%d" % qp_, qhist_g[g]), ("sT", qn_pe[par]),
                             ("sA", KN_A), ("sV", qn_chain[0])))
        do("tensor", lambda e, par=par: e.matmul(
            PA[0:1, 0:512], ones_bf1[:, 0:1], qsq[par][:, :],
            start=True, stop=True), inc="sT",
           waits=(("sA", asq),))
        qn_pe[par] = cnt("sT")
        aln = do("scalar", lambda e: e.activation(
            nrow_f[0:1, :], PA[0:1, 0:512], AF.Ln, bias=0.0, scale=1.0),
            inc="sA", waits=(("sT", qn_pe[par]),))
        am = do("scalar", lambda e: e.mul(nrow_f[0:1, :], nrow_f[0:1, :], 0.5),
                inc="sA", waits=(("sA", aln),))
        ae = do("scalar", lambda e, H=H: e.activation(
            nrow_f[0:1, :], nrow_f[0:1, :], AF.Exp,
            bias=stats[0:1, 16 + H:17 + H], scale=1.0),
            inc="sA", waits=(("sA", am),))
        nr = do("vector", lambda e, g=g: e.tensor_scalar_mul(
            nmxT_sb[g % 4][0:1, :], nrow_f[0:1, :], -1.0),
            inc="sV", waits=(("sA", ae), ("sT", nmxT_used[g % 4])))
        qn_chain[0] = nr
        nmx_ready[g % 4] = nr

    def mk_ph2(g):
        tb_, hg_ = divmod(g, NQH)
        H = hg_ // 4
        qp_ = tb_ % 2
        gp = g % 4
        sp = g % 2
        nkc = tb_ * 4 + 4
        pvb = pvbank[g % 2]
        sums_ap = sums_banks[g % 2]
        pend_pv = []

        def emit_pv(kc, pb, j0, cw):
            wpv = (("sA", stbank[kc % 3]["exp"]),)
            if kc == 0:
                wpv = wpv + (("sV", pvb["free"]), ("sA", bc_last[g % 2]))
            do("tensor", lambda e, kc=kc, H=H, pb=pb, j0=j0, cw=cw, nkc=nkc,
               a=pvb["ap"]: e.matmul(
                   a[:, j0 * 128: j0 * 128 + cw],
                   vtm[:, kc * 1024 + H * 128: kc * 1024 + (H + 1) * 128],
                   ptb[pb][:, j0 * 128: j0 * 128 + cw],
                   start=(kc == 0), stop=(kc == nkc - 1)),
               inc=None, waits=wpv)
            do("tensor", lambda e, kc=kc, pb=pb, j0=j0, cw=cw, nkc=nkc:
               e.matmul(sums_ap[0:1, j0 * 128: j0 * 128 + cw], ones_bf1[:, 0:1],
                        ptb[pb][:, j0 * 128: j0 * 128 + cw],
                        start=(kc == 0), stop=(kc == nkc - 1)),
               inc="sT")
            pbuf_user[pb] = cnt("sT")

        def emit_st(kc):
            j0 = max(0, kc - tb_ * 4)
            cw = 512 - j0 * 128
            diag = kc >= tb_ * 4
            bk = stbank[kc % 3]
            wts = ()
            if kc == 0:
                wts = (("sV", nmx_ready[gp]), ("sV", CONSTV),
                       ("sKV", KVTOT), ("sT", VT_T))
            if bk["exp"]:
                wts = wts + (("sA", bk["exp"]),)
            do("tensor", lambda e, kc=kc, j0=j0, cw=cw, H=H, hg_=hg_, qp_=qp_,
               a=bk["ap"]: e.matmul(
                   a[:, j0 * 128: j0 * 128 + cw],
                   k_sb[:, H * S + kc * 128: H * S + (kc + 1) * 128],
                   q_sb[qp_][:, hg_ * 512 + j0 * 128: hg_ * 512 + 512],
                   start=True, stop=False),
               inc=None, waits=wts)
            if diag:
                do("tensor", lambda e, j0=j0, a=bk["ap"]: e.matmul(
                    a[:, j0 * 128:(j0 + 1) * 128], ident[:, :], maskT_sb[:, :],
                    start=False, stop=False), inc=None)
            do("tensor", lambda e, j0=j0, cw=cw, gp=gp, a=bk["ap"]: e.matmul(
                a[:, j0 * 128: j0 * 128 + cw], ones_bf[0:1, 0:128],
                nmxT_sb[gp][0:1, j0 * 128: 512],
                start=False, stop=True),
               inc="sT" if kc % 2 == 1 or kc == nkc - 1 else None)
            return (kc, kc % NPB, j0, cw)

        kc = 0
        pend_exp = []
        while kc < nkc:
            pair = [emit_st(kc)]
            kc += 1
            if kc < nkc:
                pair.append(emit_st(kc))
                kc += 1
            pe_st = cnt("sT")
            for (kci, pb, j0, cw) in pair:
                bk = stbank[kci % 3]
                wex = (("sT", pe_st),) if kci == pair[0][0] else ()
                if pbuf_user[pb]:
                    wex = wex + (("sT", pbuf_user[pb]),)
                bk["exp"] = do("scalar", lambda e, pb=pb, j0=j0, cw=cw, a=bk["ap"]:
                               e.activation(ptb[pb][:, j0 * 128: j0 * 128 + cw],
                                            a[:, j0 * 128: j0 * 128 + cw], AF.Exp,
                                            bias=0.0, scale=1.0),
                               inc="sA", waits=wex)
                pend_pv.append((kci, pb, j0, cw))
            while len(pend_pv) > 2:
                emit_pv(*pend_pv.pop(0))
        while pend_pv:
            emit_pv(*pend_pv.pop(0))
        pe_grp = cnt("sT")
        nmxT_used[gp] = pe_grp
        a1 = do("scalar", lambda e, sp=sp: e.activation(
            recT_sb[sp][0:1, :], sums_ap[0:1, 0:512], AF.Ln, bias=0.0, scale=1.0),
            inc="sA", waits=(("sT", pe_grp),))
        a2 = do("scalar", lambda e, sp=sp: e.mul(
            recT_sb[sp][0:1, :], recT_sb[sp][0:1, :], -1.0), inc="sA",
            waits=(("sA", a1),))
        a3 = do("scalar", lambda e, sp=sp: e.activation(
            recT_sb[sp][0:1, :], recT_sb[sp][0:1, :], AF.Exp, bias=0.0, scale=1.0),
            inc="sA", waits=(("sA", a2),))
        do("tensor", lambda e, sp=sp: e.matmul(
            sums_ap[:, 0:512], ones_f[:, :], recT_sb[sp][0:1, :],
            start=True, stop=True), inc="sT", waits=(("sA", a3),))
        pe_bc = cnt("sT")
        bc = do("scalar", lambda e, sp=sp: CP(e, bcast_sb[sp][:, :],
                                              sums_ap[:, 0:512]),
                inc="sA", waits=(("sT", pe_bc), ("sV", bcast_used[sp])))
        bc_last[g % 2] = bc
        wev = (("sA", bc),)
        if aspill[sp]:
            wev = wev + (("sQA%d" % sp, aspill[sp]),)
        ve = do("vector", lambda e, sp=sp, a=pvb["ap"]: e.tensor_mul(
            astg[sp][:, :], a[:, 0:512], bcast_sb[sp][:, :]),
            inc="sV", waits=wev)
        pvb["free"] = ve
        bcast_used[sp] = ve
        do("sync", lambda e, sp=sp, hg_=hg_, tb_=tb_: e.dma_start(
            out=As[:, hg_ * S + tb_ * 512: hg_ * S + (tb_ + 1) * 512],
            in_=astg[sp][:, :]),
           inc="sQA%d" % sp, n=16, waits=(("sV", ve),))
        aspill[sp] += 16

    qhist_g = {}
    next_nmx = [0]
    for tb in range(TBS):
        qp = tb % 2
        if tb == 0:
            qhist[0] = qload[0]
        if tb + 1 < TBS:
            wts = (("sT", pe_tb_done[tb - 1]),) if tb >= 1 else (
                ("sA", VT_A), ("sV", VT_V), ("sT", VT_T))
            do("scalar", lambda e, tb=tb: e.dma_start(
                out=q_sb[(tb + 1) % 2][:, :],
                in_=Qs[:, (tb + 1) * 16384:(tb + 2) * 16384]),
               inc="sQL%d" % ((tb + 1) % 2), n=16, waits=wts)
            qload[(tb + 1) % 2] += 16
            qhist[tb + 1] = qload[(tb + 1) % 2]
        for hg in range(NQH):
            g = tb * NQH + hg
            while next_nmx[0] < NG and next_nmx[0] <= g + 2:
                gn = next_nmx[0]
                tbn = gn // NQH
                if tbn > tb + 1 or (tbn == tb + 1 and tb + 1 >= TBS):
                    break
                qhist_g[gn] = qhist[tbn]
                mk_nmx(gn)
                next_nmx[0] += 1
            mk_ph2(g)
        pe_tb_done[tb] = cnt("sT")
    wait("sync", ("sQA0", aspill[0]), ("sQA1", aspill[1]))
    kb.barrier()

    sz = NQH * S // 4
    for i in range(4):
        do("scalar", lambda e, i=i, sz=sz: e.dma_start(
            out=arena[:, i * sz:(i + 1) * sz], in_=As[:, i * sz:(i + 1) * sz]),
           inc="sAL", n=16)
    ALTOT = cnt("sAL")
    wfl = [0] * 5
    wuser = [0] * 5
    o2sp = [0, 0]
    for fb in range(8):
        csl = [(fb * 4 + c) % 5 for c in range(4)]
        for c in range(4):
            s5 = csl[c]
            wts = (("sT", wuser[s5]),) if wuser[s5] else ()
            do("gpsimd", lambda e, s5=s5, fb=fb, c=c: e.dma_start(
                out=wslot[s5][:, :], in_=wout[fb, c]),
               inc="sWF%d" % s5, n=16, waits=wts)
            wfl[s5] = cnt("sWF%d" % s5)
        ldw = tuple(("sWF%d" % csl[j], wfl[csl[j]]) for j in range(4))
        for t in range(NT):
            sl = next_slot()
            for c in range(4):
                s5 = csl[c]
                for eci in range(8):
                    ec = c * 8 + eci
                    wts = ()
                    if ec == 0:
                        wts = (("sAL", ALTOT),) + ldw
                        if sl["ev_done"]:
                            wts = wts + (sl["ev_done"],)
                    do("tensor", lambda e, s5=s5, ec=ec, eci=eci, t=t, a=sl["ap"]:
                       e.matmul(a[:, :],
                                arena[:, ec * S + t * 128: ec * S + (t + 1) * 128],
                                wslot[s5][:, eci * 512:(eci + 1) * 512],
                                start=(ec == 0), stop=(ec == NEC - 1)),
                       inc="sT" if ec == NEC - 1 else None, waits=wts)
            sl["pe_done"] = cnt("sT")
            op = (fb * NT + t) % 2
            wev = (("sT", sl["pe_done"]),)
            if o2sp[op]:
                wev = wev + (("sO2%d" % op, o2sp[op]),)
            ev = do(sl["eng"], lambda e, a=sl["ap"], op=op: CP(
                e, o2stg[op][:, :], a[:, :]), inc=ES[sl["eng"]], waits=wev)
            sl["ev_done"] = (ES[sl["eng"]], ev)
            do("sync", lambda e, op=op, fb=fb, t=t: e.dma_start(
                out=out2s[t * 128:(t + 1) * 128, fb * 512:(fb + 1) * 512],
                in_=o2stg[op][:, :]),
               inc="sO2%d" % op, n=16, waits=((ES[sl["eng"]], ev),))
            o2sp[op] += 16
        fbdone = cnt("sT")
        for c in range(4):
            wuser[csl[c]] = fbdone
    wait("sync", ("sO20", o2sp[0]), ("sO21", o2sp[1]))
    kb.barrier()

    x2w = 0
    prev = dict(v=0, a=0, t=0)
    routev = 0
    for t in range(NT):
        lw = ()
        if t >= 1:
            lw = (("sV", prev["v"]), ("sA", prev["a"]), ("sT", prev["t"]),
                  ("sX2W", x2w))
        do("scalar", lambda e, t=t: e.dma_start(
            out=o2_t[:, :], in_=out2s[t * 128:(t + 1) * 128, :]),
           inc="sL5", n=16, waits=lw)
        do("scalar", lambda e, t=t: e.dma_start(
            out=x5_t[:, :], in_=xin[t * 128:(t + 1) * 128, :]), inc="sL5", n=16)
        L5 = cnt("sL5")
        negm, rstd, ar = ln_stats(o2_t[:, :], 0, (("sL5", L5),),
                                  ((("sT", prev["t"]),) if t >= 1 else ()),
                                  dmp=xln3[:, :])
        v1 = do("vector", lambda e, negm=negm, rstd=rstd: e.tensor_scalar(
            ln2tmp[:, :], o2_t[:, :], negm, rstd, op0=ALU.add, op1=ALU.mult),
            inc="sV", waits=(("sA", ar), ("sL5", L5)))
        v2 = do("vector", lambda e: e.tensor_add(ln2tmp[:, :], ln2tmp[:, :], x5_t[:, :]),
                inc="sV", waits=(("sV", v1),))
        do("sync", lambda e, t=t: e.dma_start(
            out=x2s[t * 128:(t + 1) * 128, :], in_=ln2tmp[:, :]),
           inc="sX2W", n=16, waits=(("sV", v2),))
        x2w = cnt("sX2W")
        negm3, rstd3, ar3 = ln_stats(ln2tmp[:, :], 4, (("sV", v2),), dmp=xln3[:, :])
        v3 = do("vector", lambda e, negm3=negm3, rstd3=rstd3: e.tensor_scalar(
            xln3[:, :], ln2tmp[:, :], negm3, rstd3, op0=ALU.add, op1=ALU.mult),
            inc="sV", waits=(("sA", ar3),))
        for g in range(8):
            sl = next_slot()
            for j in range(4):
                fc = g * 4 + j
                wts = ()
                if j == 0:
                    wts = (("sV", v3),)
                    if sl["ev_done"]:
                        wts = wts + (sl["ev_done"],)
                do("tensor", lambda e, fc=fc, j=j, a=sl["ap"]: e.matmul(
                    a[:, j * 128:(j + 1) * 128], xln3[:, fc * 128:(fc + 1) * 128],
                    ident[:, :], start=True, stop=True),
                   inc="sT" if j == 3 else None, waits=wts)
            sl["pe_done"] = cnt("sT")
            evac(sl, arena_dst(g, t))
        aev_pairs = tuple(sl["ev_done"] for sl in SLOTS if sl["ev_done"])
        for ec in range(NEC):
            wts = ()
            if ec == 0:
                wts = aev_pairs + (("sCg", CGTOT),) + ((("sA", routev),) if routev else ())
            do("tensor", lambda e, ec=ec, t=t: e.matmul(
                PE2[:, 0:8], arena[:, ec * S + t * 128: ec * S + (t + 1) * 128],
                gate_sb[:, ec * 8:(ec + 1) * 8],
                start=(ec == 0), stop=(ec == NEC - 1)),
               inc="sT" if ec == NEC - 1 else None, waits=wts)
        pe_rt = cnt("sT")
        vm = do("vector", lambda e: e.reduce_max(
            stats[:, 8:9], PE2[:, 0:8], axis=AX.X, negate=True),
            inc="sV", waits=(("sT", pe_rt),))
        ae = do("scalar", lambda e: e.activation(
            routf[:, :], PE2[:, 0:8], AF.Exp, bias=stats[:, 8:9], scale=1.0,
            accum_out=stats[:, 9:10]), inc="sA", waits=(("sV", vm),))
        vrc = do("vector", lambda e: e.reciprocal(stats[:, 10:11], stats[:, 9:10]),
                 inc="sV", waits=(("sA", ae),))
        vn = do("vector", lambda e: e.tensor_scalar_mul(
            routb[:, :], routf[:, :], stats[:, 10:11]), inc="sV", waits=(("sV", vrc),))
        do("tensor", lambda e: e.matmul(PF[0:8, 0:128], routb[:, :], ident[:, :],
                                        start=True, stop=True),
           inc="sT", waits=(("sV", vn),) + ((("sA", routev),) if routev else ()))
        pe_rt2 = cnt("sT")
        routev = do("scalar", lambda e, t=t: CP(
            e, routT[0:8, t * 128:(t + 1) * 128], PF[0:8, 0:128]),
            inc="sA", waits=(("sT", pe_rt2),))
        prev = dict(v=cnt("sV"), a=routev, t=cnt("sT"))
    kb.barrier()

    for fc in range(NEC):
        for tb in range(TBS):
            sl = next_slot()
            wts = (("sCg", CGTOT),)
            if sl["ev_done"]:
                wts = wts + (sl["ev_done"],)
            do("tensor", lambda e, fc=fc, tb=tb, a=sl["ap"]: e.matmul(
                a[:, :], ics_sb[:, fc * 128:(fc + 1) * 128],
                routT[:, tb * 512:(tb + 1) * 512], start=True, stop=True),
               inc="sT", waits=wts)
            sl["pe_done"] = cnt("sT")
            c_ = do("vector", lambda e, fc=fc, tb=tb, a=sl["ap"]: e.tensor_mul(
                arena[:, fc * S + tb * 512: fc * S + (tb + 1) * 512],
                arena[:, fc * S + tb * 512: fc * S + (tb + 1) * 512], a[:, :]),
                inc="sV", waits=(("sT", sl["pe_done"]),))
            sl["ev_done"] = ("sV", c_)
    XS_V = cnt("sV")
    wfl = [0] * 5
    wuser = [0] * 5
    wsgn = [0] * 5
    ysp = [0, 0]
    os_used = [0, 0]
    osb = [dict(ap=PE2, ev=0), dict(ap=PF, ev=0)]
    for ob in range(8):
        csl = [(ob * 4 + c) % 5 for c in range(4)]
        for c in range(4):
            s5 = csl[c]
            wts = (("sT", wuser[s5]),) if wuser[s5] else ()
            do("gpsimd", lambda e, s5=s5, ob=ob, c=c: e.dma_start(
                out=wslot[s5][:, :], in_=wffn[ob, c]),
               inc="sWF%d" % s5, n=16, waits=wts)
            wfl[s5] = cnt("sWF%d" % s5)
            wsgn[s5] = do("scalar", lambda e, s5=s5: e.activation(
                wslot[s5][:, :], wslot[s5][:, :], AF.Sign, bias=0.0, scale=1.0),
                inc="sA", waits=(("sWF%d" % s5, wfl[s5]),))
        sgw = tuple(("sA", wsgn[csl[j]]) for j in range(4))
        for t in range(NT):
            opar = t % 2
            ob_b = osb[opar]
            wts = ()
            if ob_b["ev"]:
                wts = (("sA", ob_b["ev"]),)
            do("tensor", lambda e, t=t, ob=ob, a=ob_b["ap"]: e.matmul(
                a[:, :], routT[:, t * 128:(t + 1) * 128],
                ocs_sb[:, ob * 512:(ob + 1) * 512], start=True, stop=True),
               inc="sT", waits=wts)
            pe_os = cnt("sT")
            aos = do("scalar", lambda e, opar=opar, a=ob_b["ap"]: CP(
                e, os_sb[opar][:, :], a[:, :]), inc="sA",
                waits=(("sT", pe_os), ("sV", os_used[opar])))
            ob_b["ev"] = aos
            sl = next_slot()
            for c in range(4):
                s5 = csl[c]
                for fci in range(8):
                    fc = c * 8 + fci
                    wts = ()
                    if fc == 0:
                        wts = (("sV", XS_V),) + sgw
                        if sl["ev_done"]:
                            wts = wts + (sl["ev_done"],)
                    do("tensor", lambda e, s5=s5, fc=fc, fci=fci, t=t, a=sl["ap"]:
                       e.matmul(a[:, :],
                                arena[:, fc * S + t * 128: fc * S + (t + 1) * 128],
                                wslot[s5][:, fci * 512:(fci + 1) * 512],
                                start=(fc == 0), stop=(fc == NEC - 1)),
                       inc="sT" if fc == NEC - 1 else None, waits=wts)
            sl["pe_done"] = cnt("sT")
            yp = (ob * NT + t) % 2
            wev = (("sT", sl["pe_done"]), ("sA", aos))
            if ysp[yp]:
                wev = wev + (("sYW%d" % yp, ysp[yp]),)
            ev = do("vector", lambda e, yp=yp, opar=opar, a=sl["ap"]: e.tensor_mul(
                ystg[yp][:, :], a[:, :], os_sb[opar][:, :]), inc="sV", waits=wev)
            os_used[opar] = ev
            sl["ev_done"] = ("sV", ev)
            do("sync", lambda e, yp=yp, ob=ob, t=t: e.dma_start(
                out=ys[t * 128:(t + 1) * 128, ob * 512:(ob + 1) * 512],
                in_=ystg[yp][:, :]),
               inc="sYW%d" % yp, n=16, waits=(("sV", ev),))
            ysp[yp] += 16
        fbdone = cnt("sT")
        for c in range(4):
            wuser[csl[c]] = fbdone
    wait("sync", ("sYW0", ysp[0]), ("sYW1", ysp[1]))
    kb.barrier()

    do("scalar", lambda e: e.dma_start(out=bias_sb[:, :], in_=bias_in[:, :]),
       inc="sL7", n=16)
    bias_ev = 0
    for j in range(8):
        a = PA[:, 0:512] if j % 2 == 0 else PA[:, 512:1024]
        wts = (("sL7", 16),)
        if j >= 2:
            wts = wts + (("sA", bias_ev),)
        do("tensor", lambda e, j=j, a=a: e.matmul(
            a, ones_f[:, :], bias_sb[:, j * 512:(j + 1) * 512], start=True, stop=True),
           inc="sT", waits=wts + (("sV", CONSTV),))
        pe_b = cnt("sT")
        bias_ev = do("scalar", lambda e, j=j, a=a: CP(
            e, bias_bc[:, j * 512:(j + 1) * 512], a), inc="sA", waits=(("sT", pe_b),))
    outw = 0
    pv7 = 0
    for t in range(NT):
        lw = (("sV", pv7),) if t >= 1 else ()
        do("scalar", lambda e, t=t: e.dma_start(
            out=y7_t[:, :], in_=ys[t * 128:(t + 1) * 128, :]),
           inc="sL7", n=16, waits=lw)
        do("scalar", lambda e, t=t: e.dma_start(
            out=x27_t[:, :], in_=x2s[t * 128:(t + 1) * 128, :]), inc="sL7", n=16)
        L7 = cnt("sL7")
        wts = (("sL7", L7), ("sA", bias_ev))
        if outw:
            wts = wts + (("sOUT", outw),)
        v1 = do("vector", lambda e: e.tensor_add(y1_t[:, :], y7_t[:, :], bias_bc[:, :]),
                inc="sV", waits=wts)
        negm, rstd, ar = ln_stats(y1_t[:, :], 0, (("sV", v1),), dmp=y7_t[:, :])
        v2 = do("vector", lambda e, negm=negm, rstd=rstd: e.tensor_scalar(
            y1_t[:, :], y1_t[:, :], negm, rstd, op0=ALU.add, op1=ALU.mult),
            inc="sV", waits=(("sA", ar),))
        v3 = do("vector", lambda e: e.tensor_add(y1_t[:, :], y1_t[:, :], x27_t[:, :]),
                inc="sV", waits=(("sV", v2),))
        pv7 = v3
        do("sync", lambda e, t=t: e.dma_start(
            out=out[t * 128:(t + 1) * 128, :], in_=y1_t[:, :]),
           inc="sOUT", n=16, waits=(("sV", v3),))
        outw = cnt("sOUT")
    wait("sync", ("sOUT", outw))
    kb.barrier()

    with nc.Block() as block:
        block.sync(lambda e: [f(e) for f in kb.ops["sync"]])
        block.scalar(lambda e: [f(e) for f in kb.ops["scalar"]])
        block.vector(lambda e: [f(e) for f in kb.ops["vector"]])
        block.tensor(lambda e: [f(e) for f in kb.ops["tensor"]])
        block.gpsimd(lambda e: [f(e) for f in kb.ops["gpsimd"]])
    return nc


def stage_shared(w_in, w_out, w_ffn, gate_w, ics, ocs, bias, NT=16):
    S = NT * 128
    f32 = np.float32
    wqkv = np.empty((NFC, 128, NEC * 128), f32)
    wt = np.ascontiguousarray(w_in.T)
    for fc in range(NFC):
        r = _frow(fc)
        blk = wt[:, r:r + 128].reshape(NEC, 128, 128)
        wqkv[fc] = blk.transpose(1, 0, 2).reshape(128, NEC * 128)

    def stage_w(w):
        st = np.empty((8, 4, 128, 8 * 512), f32)
        wtt = np.ascontiguousarray(w.T)
        for b in range(8):
            blk = wtt[:, b * 512:(b + 1) * 512].reshape(NEC, 128, 512)
            for c in range(4):
                st[b, c] = blk[c * 8:(c + 1) * 8].transpose(1, 0, 2).reshape(128, 8 * 512)
        return st

    wouts = stage_w(w_out)
    wffns = stage_w(w_ffn)
    gate_stg = np.ascontiguousarray(
        gate_w.T.reshape(NEC, 128, 8).transpose(1, 0, 2).reshape(128, NEC * 8)).astype(f32)
    inv = ROPE_BASE ** (-np.arange(0, D, 2, dtype=f32) / D)
    ang = np.arange(S, dtype=f32)[:, None] * inv[None, :].astype(f32)
    c = np.cos(ang).astype(f32).T
    s = np.sin(ang).astype(f32).T
    alpha = np.float32(1.0 / np.sqrt(np.float32(D)))
    cq = np.concatenate([c, c], 0) * alpha
    sq = np.concatenate([s, s], 0) * alpha
    ck = np.concatenate([c, c], 0)
    sk = np.concatenate([s, s], 0)
    ident = np.eye(128, dtype=BF16NP)
    rot = np.zeros((128, 128), dtype=BF16NP)
    rot[0:64, 64:128] = np.eye(64, dtype=BF16NP)
    rot[64:128, 0:64] = -np.eye(64, dtype=BF16NP)
    mask = np.triu(np.full((128, 128), NEGINF, dtype=f32), k=1).astype(BF16NP)
    maskT = np.tril(np.full((128, 128), NEGINF, dtype=f32), k=-1).astype(BF16NP)
    return dict(
        wqkv=wqkv, wout=wouts, wffn=wffns, gate_st=gate_stg,
        ics_in=np.ascontiguousarray(ics, f32), ocs_in=np.ascontiguousarray(ocs, f32),
        bias_in=np.ascontiguousarray(bias, f32).reshape(1, E),
        ident_in=ident, rot_in=rot, mask_in=mask, maskT_in=maskT,
        cq_in=np.ascontiguousarray(cq), sq_in=np.ascontiguousarray(sq),
        ck_in=np.ascontiguousarray(ck), sk_in=np.ascontiguousarray(sk),
    )


def kernel(L0_attn_x, L0_attn_w_in, L0_attn_w_out, L0_ffn_weight, L0_ffn_bias,
           L0_ffn_gate_w, L0_ffn_in_channel_scale, L0_ffn_out_channel_scale):
    NT = 16
    x = np.ascontiguousarray(np.asarray(L0_attn_x, dtype=np.float32))
    B = x.shape[0]
    shared = stage_shared(np.asarray(L0_attn_w_in, np.float32),
                          np.asarray(L0_attn_w_out, np.float32),
                          np.asarray(L0_ffn_weight, np.float32),
                          np.asarray(L0_ffn_gate_w, np.float32),
                          np.asarray(L0_ffn_in_channel_scale, np.float32),
                          np.asarray(L0_ffn_out_channel_scale, np.float32),
                          np.asarray(L0_ffn_bias, np.float32), NT=NT)
    nc = build_program(NT=NT)
    in_maps = []
    for b in range(B):
        m = dict(shared)
        m["xin"] = np.ascontiguousarray(x[b])
        in_maps.append(m)
    res = run_bass_kernel_spmd(nc, in_maps, core_ids=list(range(B)))
    return np.stack([res.results[b]["out"] for b in range(B)], axis=0)


# version 11
# speedup vs baseline: 1.0812x; 1.0812x over previous
import numpy as np
import ml_dtypes
import concourse.bass as bass
import concourse.mybir as mybir
from concourse.bass_utils import run_bass_kernel_spmd

F32 = mybir.dt.float32
BF16 = mybir.dt.bfloat16
AF = mybir.ActivationFunctionType
ALU = mybir.AluOpType
AX = mybir.AxisListType
BF16NP = ml_dtypes.bfloat16

E = 4096
NQH = 32
NKVH = 8
D = 128
NEC = 32
NFC = 48
ROPE_BASE = 10000.0
EPS = 1e-5
NEGINF = -1.0e30



def CP(e, d, s):
    if isinstance(e, bass.BassScalarEngine):
        return e.copy(d, s)
    return e.tensor_copy(d, s)


def _frow(fc):
    if fc < 32:
        return (fc // 8) * 1024 + (fc % 8) * 128
    if fc < 40:
        return 4096 + (fc - 32) * 128
    return 5120 + (fc - 40) * 128


class KB:
    ENGS = ("sync", "scalar", "vector", "tensor", "gpsimd")

    def __init__(self, nc):
        self.nc = nc
        self.ops = {k: [] for k in self.ENGS}
        self.sem = {}

    def S(self, name):
        if name not in self.sem:
            self.sem[name] = [self.nc.alloc_semaphore(name), 0]
        return self.sem[name]

    def cnt(self, name):
        return self.S(name)[1]

    def do(self, eng, make, inc=None, n=1, waits=()):
        tgt = None
        if inc is not None:
            s = self.S(inc)
            s[1] += n
            tgt = (s[0], n)
        pairs = tuple((self.S(nm)[0], v) for (nm, v) in waits if v)

        def f(e, make=make, tgt=tgt, pairs=pairs):
            for (h, v) in pairs:
                e.wait_ge(h, v)
            i = make(e)
            if tgt is not None:
                i.then_inc(*tgt)

        self.ops[eng].append(f)
        return self.sem[inc][1] if inc is not None else None

    def wait(self, eng, *waits):
        pairs = tuple((self.S(nm)[0], v) for (nm, v) in waits if v)
        self.ops[eng].append(lambda e, pairs=pairs: [e.wait_ge(h, v) for (h, v) in pairs])

    def barrier(self):
        snap = tuple((s[0], s[1]) for s in self.sem.values() if s[1] > 0)
        for eng in self.ENGS:
            self.ops[eng].append(
                lambda e, snap=snap: [e.wait_ge(h, v) for (h, v) in snap])


def build_program(NT=16):
    S = NT * 128
    TBS = NT // 4
    NKC = S // 128
    nc = bass.Bass("TRN2", target_bir_lowering=False, debug=False)

    xin = nc.dram_tensor("xin", [S, E], F32, kind="ExternalInput").ap()
    wqkv = nc.dram_tensor("wqkv", [NFC, 128, NEC * 128], F32, kind="ExternalInput").ap()
    wout = nc.dram_tensor("wout", [8, 4, 128, 8 * 512], F32, kind="ExternalInput").ap()
    wffn = nc.dram_tensor("wffn", [8, 4, 128, 8 * 512], F32, kind="ExternalInput").ap()
    gate_st = nc.dram_tensor("gate_st", [128, 256], F32, kind="ExternalInput").ap()
    ics_in = nc.dram_tensor("ics_in", [8, E], F32, kind="ExternalInput").ap()
    ocs_in = nc.dram_tensor("ocs_in", [8, E], F32, kind="ExternalInput").ap()
    bias_in = nc.dram_tensor("bias_in", [1, E], F32, kind="ExternalInput").ap()
    ident_in = nc.dram_tensor("ident_in", [128, 128], BF16, kind="ExternalInput").ap()
    rot_in = nc.dram_tensor("rot_in", [128, 128], BF16, kind="ExternalInput").ap()
    mask_in = nc.dram_tensor("mask_in", [128, 128], BF16, kind="ExternalInput").ap()
    maskT_in = nc.dram_tensor("maskT_in", [128, 128], BF16, kind="ExternalInput").ap()
    cq_in = nc.dram_tensor("cq_in", [128, S], F32, kind="ExternalInput").ap()
    sq_in = nc.dram_tensor("sq_in", [128, S], F32, kind="ExternalInput").ap()
    ck_in = nc.dram_tensor("ck_in", [128, S], F32, kind="ExternalInput").ap()
    sk_in = nc.dram_tensor("sk_in", [128, S], F32, kind="ExternalInput").ap()
    out = nc.dram_tensor("out", [S, E], F32, kind="ExternalOutput").ap()

    Qs = nc.dram_tensor("Qs", [128, TBS * 16384], BF16, kind="Internal").ap()
    Ks = nc.dram_tensor("Ks", [128, NKVH * S], BF16, kind="Internal").ap()
    Vs = nc.dram_tensor("Vs", [128, NKVH * S], BF16, kind="Internal").ap()
    As = nc.dram_tensor("As", [128, NQH * S], BF16, kind="Internal").ap()
    out2s = nc.dram_tensor("out2s", [S, E], BF16, kind="Internal").ap()
    x2s = nc.dram_tensor("x2s", [S, E], F32, kind="Internal").ap()
    ys = nc.dram_tensor("ys", [S, E], BF16, kind="Internal").ap()

    BASE = 16896
    A0 = BASE
    S0 = BASE + 131072
    PP = BASE + 131072 + 57344
    al = nc.alloc_sbuf_tensor_at

    arena = al("arena", [128, NEC * S], BF16, offset=A0)
    k_sb = al("k_sb", [128, NKVH * S], BF16, offset=A0)
    vtm = al("vtm", [128, NKVH * S], BF16, offset=A0 + NKVH * S * 2)
    q_sb = [al(f"q_sb{i}", [128, 16384], BF16,
               offset=A0 + 4 * NKVH * S + i * 32768) for i in range(2)]
    bias_bc = al("bias_bc", [128, E], F32, offset=A0)
    bias_sb = al("bias_sb", [1, E], F32, offset=A0 + E * 4)
    y7b = [al(f"y7b{i}", [128, E], BF16, offset=A0 + 32768 + i * 8192) for i in range(2)]
    x27b = [al(f"x27b{i}", [128, E], F32, offset=A0 + 49152 + i * 16384) for i in range(2)]
    y1b = [al(f"y1b{i}", [128, E], F32, offset=A0 + 81920 + i * 16384) for i in range(2)]

    x_t = [al(f"x_t{i}", [128, E], F32, offset=S0 + i * 16384) for i in range(2)]
    xln_t = [al(f"xln_t{i}", [128, E], BF16, offset=S0 + 32768 + i * 8192) for i in range(2)]
    dump = al("dump", [128, E], BF16, offset=S0 + 49152)
    wq_sb = [al(f"wq_sb{i}", [128, 4096], BF16, offset=S0 + i * 8192) for i in range(2)]
    T2 = S0 + 16384
    cq_sb = al("cq_sb", [128, S], BF16, offset=T2)
    sq_sb = al("sq_sb", [128, S], BF16, offset=T2 + 2 * S)
    ck_sb = al("ck_sb", [128, S], BF16, offset=T2 + 4 * S)
    sk_sb = al("sk_sb", [128, S], BF16, offset=T2 + 6 * S)
    R2 = T2 + 8 * S
    qb_sb = [al(f"qb_sb{i}", [128, 512], BF16, offset=R2 + i * 1024) for i in range(4)]
    scr1 = [al(f"scr1_{i}", [128, 512], F32, offset=R2 + 4096 + i * 2048) for i in range(4)]
    scr2 = [al(f"scr2_{i}", [128, 512], F32, offset=R2 + 12288 + i * 2048) for i in range(4)]
    qout_sb = [al(f"qout_sb{i}", [128, 512], BF16, offset=R2 + 20480 + i * 1024)
               for i in range(4)]
    vraw = al("vraw", [128, NKVH * S], BF16, offset=A0 + 4 * NKVH * S + 32768)
    ptb = [al(f"ptb{i}", [128, 512], BF16, offset=S0 + i * 1024) for i in range(6)]
    astg = [al(f"astg{i}", [128, 512], BF16, offset=S0 + 6144 + i * 1024) for i in range(2)]
    bcast_sb = [al(f"bcast_sb{i}", [128, 512], F32, offset=S0 + 8192 + i * 2048)
                for i in range(2)]
    nmxT_sb = [al(f"nmxT_sb{i}", [1, 512], BF16, offset=S0 + 12288 + i * 1024)
               for i in range(4)]
    recT_sb = [al(f"recT_sb{i}", [1, 512], F32, offset=S0 + 16384 + i * 2048)
               for i in range(2)]
    knsq = al("knsq", [128, S], BF16, offset=S0 + 20480)
    qsq = [al(f"qsq{i}", [128, 512], BF16, offset=S0 + 20480 + 2 * S + i * 1024)
           for i in range(2)]
    nrow_f = al("nrow_f", [1, 512], F32, offset=S0 + 20480 + 2 * S + 2048)
    wslot = [al(f"wslot{i}", [128, 4096], BF16, offset=S0 + i * 8192) for i in range(5)]
    os_sb = [al(f"os_sb{i}", [128, 512], BF16, offset=S0 + 40960 + i * 1024) for i in range(2)]
    ystg = [al(f"ystg{i}", [128, 512], BF16, offset=S0 + 43008 + i * 1024) for i in range(2)]
    o2stg = [al(f"o2stg{i}", [128, 512], BF16, offset=S0 + 45056 + i * 1024) for i in range(2)]
    o2_t = al("o2_t", [128, E], BF16, offset=S0)
    x5_t = al("x5_t", [128, E], F32, offset=S0 + 8192)
    ln2tmp = al("ln2tmp", [128, E], F32, offset=S0 + 24576)
    xln3 = al("xln3", [128, E], BF16, offset=S0 + 40960)
    y7_t = al("y7_t", [128, E], BF16, offset=S0)
    x27_t = al("x27_t", [128, E], F32, offset=S0 + 8192)
    y1_t = al("y1_t", [128, E], F32, offset=S0 + 24576)
    ident = al("ident", [128, 128], BF16, offset=PP)
    rot = al("rot", [128, 128], BF16, offset=PP + 256)
    ones_f = al("ones_f", [1, 128], F32, offset=PP + 512)
    ones_bf = al("ones_bf", [1, 128], BF16, offset=PP + 20480)
    ones_bf1 = al("ones_bf1", [128, 1], BF16, offset=PP + 20736)
    maskT_sb = al("maskT_sb", [128, 128], BF16, offset=PP + 20800)
    mask_sb = al("mask_sb", [128, 128], BF16, offset=PP + 1024)
    gate_sb = al("gate_sb", [128, 256], BF16, offset=PP + 1536)
    ics_sb = al("ics_sb", [8, E], BF16, offset=PP + 2048)
    ocs_sb = al("ocs_sb", [8, E], BF16, offset=PP + 10240)
    routT = al("routT", [8, S], BF16, offset=PP + 18432)
    stats = al("stats", [128, 32], F32, offset=PP + 18432 + 2 * S)
    routf = al("routf", [128, 8], F32, offset=PP + 18432 + 2 * S + 128)
    routb = al("routb", [128, 8], BF16, offset=PP + 18432 + 2 * S + 160)
    consts = al("consts", [128, 4], F32, offset=PP + 18432 + 2 * S + 192)

    PA = nc.alloc_psum_tensor("PA", [128, 1024], F32)
    PB = nc.alloc_psum_tensor("PB", [128, 1024], F32)
    PC = nc.alloc_psum_tensor("PC", [128, 512], F32)
    PD = nc.alloc_psum_tensor("PD", [128, 512], F32)
    PE2 = nc.alloc_psum_tensor("PE2", [128, 512], F32)
    PF = nc.alloc_psum_tensor("PF", [128, 512], F32)
    SLOTS = [
        dict(ap=PA[:, 0:512], eng="vector", pe_done=0, ev_done=0),
        dict(ap=PA[:, 512:1024], eng="scalar", pe_done=0, ev_done=0),
        dict(ap=PB[:, 0:512], eng="vector", pe_done=0, ev_done=0),
        dict(ap=PB[:, 512:1024], eng="scalar", pe_done=0, ev_done=0),
        dict(ap=PC[:, :], eng="vector", pe_done=0, ev_done=0),
        dict(ap=PD[:, :], eng="scalar", pe_done=0, ev_done=0),
    ]

    kb = KB(nc)
    do, wait, cnt = kb.do, kb.wait, kb.cnt
    ES = {"vector": "sV", "scalar": "sA", "tensor": "sT", "gpsimd": "sP"}
    slot_i = [0]

    def next_slot():
        sl = SLOTS[slot_i[0] % 6]
        slot_i[0] += 1
        return sl

    def evac(sl, dst_ap, extra=()):
        eng = sl["eng"]
        c = do(eng, lambda e, a=sl["ap"], d=dst_ap: CP(e, d, a[:, :]),
               inc=ES[eng], waits=(("sT", sl["pe_done"]),) + tuple(extra))
        sl["ev_done"] = (ES[eng], c)
        return c

    def arena_dst(g, t):
        return arena.ap().rearrange("p (a b) -> p a b", b=S)[
            :, g * 4:g * 4 + 4, t * 128:(t + 1) * 128]

    def ln_stats(src_ap, c, pre_waits_v, pre_waits_a=(), dmp=None):
        if dmp is None:
            dmp = dump[:, :]
        vsum = do("vector", lambda e: e.reduce_sum(stats[:, c:c + 1], src_ap, axis=AX.X),
                  inc="sV", waits=pre_waits_v)
        vneg = do("vector", lambda e: e.tensor_scalar_mul(
            stats[:, c:c + 1], stats[:, c:c + 1], -1.0 / E), inc="sV", waits=(("sV", vsum),))
        asq = do("scalar", lambda e, dmp=dmp: e.activation(
            dmp, src_ap, AF.Square, bias=stats[:, c:c + 1], scale=1.0,
            accum_out=stats[:, c + 1:c + 2]), inc="sA",
            waits=(("sV", vneg),) + tuple(pre_waits_a))
        am = do("scalar", lambda e: e.mul(stats[:, c + 1:c + 2], stats[:, c + 1:c + 2], 1.0 / E),
                inc="sA", waits=(("sA", asq),))
        al_ = do("scalar", lambda e: e.activation(
            stats[:, c + 2:c + 3], stats[:, c + 1:c + 2], AF.Ln,
            bias=consts[:, 0:1], scale=1.0), inc="sA", waits=(("sA", am),))
        am2 = do("scalar", lambda e: e.mul(stats[:, c + 2:c + 3], stats[:, c + 2:c + 3], -0.5),
                 inc="sA", waits=(("sA", al_),))
        ar = do("scalar", lambda e: e.activation(
            stats[:, c + 3:c + 4], stats[:, c + 2:c + 3], AF.Exp, bias=0.0, scale=1.0),
            inc="sA", waits=(("sA", am2),))
        return stats[:, c:c + 1], stats[:, c + 3:c + 4], ar

    for dst, src in ((ident[:, :], ident_in[:, :]), (rot[:, :], rot_in[:, :]),
                     (mask_sb[:, :], mask_in[:, :]), (maskT_sb[:, :], maskT_in[:, :])):
        do("sync", lambda e, d=dst, s=src: e.dma_start(out=d, in_=s), inc="sC", n=16)
    for dst, src in ((gate_sb[:, :], gate_st[:, :]), (ics_sb[:, :], ics_in[:, :]),
                     (ocs_sb[:, :], ocs_in[:, :])):
        do("gpsimd", lambda e, d=dst, s=src: e.dma_start(out=d, in_=s), inc="sCg", n=16)
    CTOT = cnt("sC")
    CGTOT = cnt("sCg")
    do("vector", lambda e: e.memset(consts[:, 0:1], EPS), inc="sV")
    do("vector", lambda e: e.memset(ones_f[:, :], 1.0), inc="sV")
    do("vector", lambda e: e.memset(ones_bf[:, :], 1.0), inc="sV")
    do("vector", lambda e: e.memset(ones_bf1[:, :], 1.0), inc="sV")
    CONSTV = cnt("sV")

    xload = [0, 0]
    pe_done_tile = {}
    for t in range(NT):
        p = t % 2
        w = (("sT", pe_done_tile[t - 2]),) if t >= 2 else ()
        do("scalar", lambda e, p=p, t=t: e.dma_start(
            out=x_t[p][:, :], in_=xin[t * 128:(t + 1) * 128, :]),
            inc="sX%d" % p, n=16, waits=w)
        xload[p] += 16
        c0 = 0 if p == 0 else 4
        negm, rstd, ar = ln_stats(x_t[p][:, :], c0,
                                  (("sX%d" % p, xload[p]), ("sV", CONSTV)))
        wxln = (("sA", ar),)
        if t >= 2:
            wxln = wxln + (("sT", pe_done_tile[t - 2]),)
        vts = do("vector", lambda e, p=p, negm=negm, rstd=rstd: e.tensor_scalar(
            xln_t[p][:, :], x_t[p][:, :], negm, rstd, op0=ALU.add, op1=ALU.mult),
            inc="sV", waits=wxln)
        for g in range(8):
            sl = next_slot()
            for j in range(4):
                fc = g * 4 + j
                wts = ()
                if j == 0:
                    wts = (("sV", vts), ("sC", CTOT))
                    if sl["ev_done"]:
                        wts = wts + (sl["ev_done"],)
                do("tensor", lambda e, p=p, fc=fc, j=j, a=sl["ap"]: e.matmul(
                    a[:, j * 128:(j + 1) * 128], xln_t[p][:, fc * 128:(fc + 1) * 128],
                    ident[:, :], start=True, stop=True),
                   inc="sT" if j == 3 else None, waits=wts)
            sl["pe_done"] = cnt("sT")
            evac(sl, arena_dst(g, t))
        pe_done_tile[t] = cnt("sT")
    kb.barrier()

    for dst, src in ((cq_sb[:, :], cq_in[:, :]), (sq_sb[:, :], sq_in[:, :]),
                     (ck_sb[:, :], ck_in[:, :]), (sk_sb[:, :], sk_in[:, :])):
        do("gpsimd", lambda e, d=dst, s=src: e.dma_start(out=d, in_=s), inc="sC2", n=16)
    C2TOT = cnt("sC2")

    wload = [0, 0]
    qkspill = [0, 0, 0, 0]
    pe_fc_done = {}
    swap_free = [0, 0]
    swap_bank = [PE2, PF]
    rope = {}
    pend = []
    chunk = 0

    def flush_rope():
        while pend:
            k = pend.pop(0)
            r = rope[k]
            cp2 = k % 4
            sp = k % 2
            wts = (("sT", r["slot_pe"]), ("sA", r["aev"]))
            if swap_free[sp]:
                wts = wts + (("sV", swap_free[sp]),)
            do("tensor", lambda e, k=k, b=swap_bank[sp]: e.matmul(
                b[:, :], rot[:, :], qb_sb[k % 4][:, :], start=True, stop=True),
               inc="sT", waits=wts)
            r["swap"] = cnt("sT")
            wts = (("sT", r["swap"]), ("sP", r["t1"]), ("sC2", C2TOT))
            if qkspill[cp2]:
                wts = wts + (("sQK%d" % cp2, qkspill[cp2]),)
            v1 = do("vector", lambda e, k=k, b=swap_bank[sp], st=r["stab"], tb=r["tb"]:
                    e.tensor_mul(scr2[k % 4][:, :], b[:, :],
                                 st[:, tb * 512:(tb + 1) * 512]),
                    inc="sV", waits=wts)
            swap_free[sp] = v1
            r["v1"] = v1
            v2 = do("vector", lambda e, k=k: e.tensor_add(
                qout_sb[k % 4][:, :], scr1[k % 4][:, :], scr2[k % 4][:, :]),
                inc="sV", waits=(("sV", v1),))
            r["v2"] = v2
            do("sync", lambda e, d=r["dst"], k=k: e.dma_start(out=d, in_=qout_sb[k % 4][:, :]),
               inc="sQK%d" % cp2, n=16, waits=(("sV", v2),))
            qkspill[cp2] += 16

    for fc in range(NFC):
        p = fc % 2
        wts = (("sT", pe_fc_done[fc - 2]),) if fc >= 2 else ()
        do("gpsimd", lambda e, p=p, fc=fc: e.dma_start(out=wq_sb[p][:, :], in_=wqkv[fc]),
           inc="sW%d" % p, n=16, waits=wts)
        wload[p] += 16
        is_v = fc >= 40
        is_q = fc < 32
        for tb in range(TBS):
            sl = next_slot()
            for ec in range(NEC):
                wts2 = ()
                if ec == 0:
                    wts2 = (("sW%d" % p, wload[p]),)
                    if sl["ev_done"]:
                        wts2 = wts2 + (sl["ev_done"],)
                do("tensor", lambda e, p=p, ec=ec, tb=tb, a=sl["ap"]: e.matmul(
                    a[:, :], wq_sb[p][:, ec * 128:(ec + 1) * 128],
                    arena[:, ec * S + tb * 512: ec * S + (tb + 1) * 512],
                    start=(ec == 0), stop=(ec == NEC - 1)),
                   inc="sT" if ec == NEC - 1 else None, waits=wts2)
            sl["pe_done"] = cnt("sT")
            if is_v:
                cp2 = chunk % 4
                h = fc - 40
                wev = (("sT", sl["pe_done"]),)
                if qkspill[cp2]:
                    wev = wev + (("sQK%d" % cp2, qkspill[cp2]),)
                ev = do(sl["eng"], lambda e, a=sl["ap"], k=chunk: CP(
                    e, qout_sb[k % 4][:, :], a[:, :]), inc=ES[sl["eng"]], waits=wev)
                sl["ev_done"] = (ES[sl["eng"]], ev)
                do("sync", lambda e, h=h, tb=tb, k=chunk: e.dma_start(
                    out=Vs[:, h * S + tb * 512: h * S + (tb + 1) * 512],
                    in_=qout_sb[k % 4][:, :]),
                   inc="sQK%d" % cp2, n=16, waits=((ES[sl["eng"]], ev),))
                qkspill[cp2] += 16
            else:
                wev = (("sT", sl["pe_done"]),)
                if chunk - 4 in rope:
                    r4 = rope[chunk - 4]
                    wev = wev + (("sT", r4["swap"]), ("sP", r4["t1"]))
                aev = do("scalar", lambda e, a=sl["ap"], k=chunk: CP(
                    e, qb_sb[k % 4][:, :], a[:, :]), inc="sA", waits=wev)
                sl["ev_done"] = ("sA", aev)
                ctab = cq_sb if is_q else ck_sb
                stab = sq_sb if is_q else sk_sb
                wp = (("sA", aev), ("sC2", C2TOT))
                if chunk - 4 in rope:
                    wp = wp + (("sV", rope[chunk - 4]["v2"]),)
                t1 = do("gpsimd", lambda e, k=chunk, ct=ctab, tb=tb: e.tensor_mul(
                    scr1[k % 4][:, :], qb_sb[k % 4][:, :],
                    ct[:, tb * 512:(tb + 1) * 512]), inc="sP", waits=wp)
                if is_q:
                    dst = Qs[:, tb * 16384 + fc * 512: tb * 16384 + (fc + 1) * 512]
                else:
                    dst = Ks[:, (fc - 32) * S + tb * 512: (fc - 32) * S + (tb + 1) * 512]
                rope[chunk] = dict(slot_pe=sl["pe_done"], aev=aev, t1=t1, tb=tb,
                                   stab=stab, dst=dst, swap=0, v1=0, v2=0)
                pend.append(chunk)
            chunk += 1
        flush_rope()
        pe_fc_done[fc] = cnt("sT")
    flush_rope()
    wait("sync", ("sQK0", qkspill[0]), ("sQK1", qkspill[1]),
         ("sQK2", qkspill[2]), ("sQK3", qkspill[3]))
    kb.barrier()

    do("scalar", lambda e: e.dma_start(out=k_sb[:, :], in_=Ks[:, :]), inc="sKV", n=16)
    do("scalar", lambda e: e.dma_start(out=vraw[:, :], in_=Vs[:, :]), inc="sKV", n=16)
    KVTOT = cnt("sKV")
    for h in range(NKVH):
        for kcg in range(NKC // 4):
            sl = next_slot()
            for j in range(4):
                kc = kcg * 4 + j
                wts = ()
                if j == 0:
                    wts = (("sKV", KVTOT),)
                    if sl["ev_done"]:
                        wts = wts + (sl["ev_done"],)
                do("tensor", lambda e, h=h, kc=kc, j=j, a=sl["ap"]: e.matmul(
                    a[:, j * 128:(j + 1) * 128],
                    vraw[:, h * S + kc * 128: h * S + (kc + 1) * 128],
                    ident[:, :], start=True, stop=True),
                   inc="sT" if j == 3 else None, waits=wts)
            sl["pe_done"] = cnt("sT")
            dst = vtm.ap().rearrange("p (a b) -> p a b", b=1024)[
                :, kcg * 4:kcg * 4 + 4, h * 128:(h + 1) * 128]
            evac(sl, dst)
    VT_T = cnt("sT")
    VT_A, VT_V = cnt("sA"), cnt("sV")

    ALPHA = float(1.0 / np.sqrt(np.float64(D)))
    kn_pe = VT_T
    for H in range(NKVH):
        asq = do("scalar", lambda e, H=H: e.activation(
            knsq[:, :], k_sb[:, H * S:(H + 1) * S], AF.Square, bias=0.0, scale=1.0),
            inc="sA", waits=(("sKV", KVTOT), ("sT", kn_pe)))
        vprev = 0
        for kcb in range(S // 512):
            do("tensor", lambda e, kcb=kcb: e.matmul(
                PA[0:1, 0:512], ones_bf1[:, 0:1], knsq[:, kcb * 512:(kcb + 1) * 512],
                start=True, stop=True),
               inc="sT", waits=(("sA", asq), ("sV", CONSTV), ("sV", VT_V),
                                ("sA", VT_A),
                                ("sV", vprev) if vprev else ("sV", 0)))
            pe_n = cnt("sT")
            vprev = do("vector", lambda e, kcb=kcb, H=H: e.reduce_max(
                stats[0:1, 24:25] if kcb == 0 else stats[0:1, 25:26],
                PA[0:1, 0:512], axis=AX.X), inc="sV", waits=(("sT", pe_n),))
            if kcb > 0:
                vprev = do("vector", lambda e: e.tensor_tensor(
                    stats[0:1, 24:25], stats[0:1, 24:25], stats[0:1, 25:26],
                    op=ALU.max), inc="sV", waits=(("sV", vprev),))
        kn_pe = cnt("sT")
        va = do("vector", lambda e: e.tensor_scalar_mul(
            stats[0:1, 24:25], stats[0:1, 24:25], ALPHA * ALPHA),
            inc="sV", waits=(("sV", vprev),))
        al_ = do("scalar", lambda e, H=H: e.activation(
            stats[0:1, 16 + H:17 + H], stats[0:1, 24:25], AF.Ln, bias=0.0, scale=1.0),
            inc="sA", waits=(("sV", va),))
        do("scalar", lambda e, H=H: e.mul(
            stats[0:1, 16 + H:17 + H], stats[0:1, 16 + H:17 + H], 0.5),
            inc="sA", waits=(("sA", al_),))
    KN_A = cnt("sA")

    stbank = [dict(ap=PB[:, 0:512], exp=0), dict(ap=PB[:, 512:1024], exp=0),
              dict(ap=PC[:, :], exp=0), dict(ap=PD[:, :], exp=0)]
    sums_banks = [PA[:, 512:1024], PA[:, 512:1024]]
    NPB = 6
    pbuf_user = [0] * 6
    nmxT_used = [0, 0, 0, 0]
    nmx_ready = [0, 0, 0, 0]
    qn_pe = [0, 0]
    qn_chain = [0]
    bcast_used = [0, 0]
    pvbank = [dict(ap=PE2, free=0), dict(ap=PF, free=0)]
    bc_last = [0, 0]
    qload = [0, 0]
    qhist = {}
    aspill = [0, 0]
    pe_tb_done = {}

    do("scalar", lambda e: e.dma_start(out=q_sb[0][:, :], in_=Qs[:, 0:16384]),
       inc="sQL0", n=16)
    qload[0] += 16

    NG = TBS * NQH

    def mk_nmx(g):
        tb_, hg_ = divmod(g, NQH)
        H = hg_ // 4
        qp_ = tb_ % 2
        par = g % 2
        asq = do("scalar", lambda e, hg_=hg_, qp_=qp_, par=par: e.activation(
            qsq[par][:, :], q_sb[qp_][:, hg_ * 512:(hg_ + 1) * 512], AF.Square,
            bias=0.0, scale=1.0),
            inc="sA", waits=(("sQL%d" % qp_, qhist_g[g]), ("sT", qn_pe[par]),
                             ("sA", KN_A), ("sV", qn_chain[0])))
        do("tensor", lambda e, par=par: e.matmul(
            PA[0:1, 0:512], ones_bf1[:, 0:1], qsq[par][:, :],
            start=True, stop=True), inc="sT",
           waits=(("sA", asq),))
        qn_pe[par] = cnt("sT")
        aln = do("scalar", lambda e: e.activation(
            nrow_f[0:1, :], PA[0:1, 0:512], AF.Ln, bias=0.0, scale=1.0),
            inc="sA", waits=(("sT", qn_pe[par]),))
        am = do("scalar", lambda e: e.mul(nrow_f[0:1, :], nrow_f[0:1, :], 0.5),
                inc="sA", waits=(("sA", aln),))
        ae = do("scalar", lambda e, H=H: e.activation(
            nrow_f[0:1, :], nrow_f[0:1, :], AF.Exp,
            bias=stats[0:1, 16 + H:17 + H], scale=1.0),
            inc="sA", waits=(("sA", am),))
        nr = do("vector", lambda e, g=g: e.tensor_scalar_mul(
            nmxT_sb[g % 4][0:1, :], nrow_f[0:1, :], -1.0),
            inc="sV", waits=(("sA", ae), ("sT", nmxT_used[g % 4])))
        qn_chain[0] = nr
        nmx_ready[g % 4] = nr

    def mk_ph2(g):
        tb_, hg_ = divmod(g, NQH)
        H = hg_ // 4
        qp_ = tb_ % 2
        gp = g % 4
        sp = g % 2
        nkc = tb_ * 4 + 4
        pvb = pvbank[g % 2]
        sums_ap = sums_banks[g % 2]
        pend_pv = []

        def emit_pv(kc, pb, j0, cw):
            wpv = (("sA", stbank[kc % 4]["exp"]),)
            if kc == 0:
                wpv = wpv + (("sV", pvb["free"]), ("sA", max(bc_last)))
            do("tensor", lambda e, kc=kc, H=H, pb=pb, j0=j0, cw=cw, nkc=nkc,
               a=pvb["ap"]: e.matmul(
                   a[:, j0 * 128: j0 * 128 + cw],
                   vtm[:, kc * 1024 + H * 128: kc * 1024 + (H + 1) * 128],
                   ptb[pb][:, j0 * 128: j0 * 128 + cw],
                   start=(kc == 0), stop=(kc == nkc - 1)),
               inc=None, waits=wpv)
            do("tensor", lambda e, kc=kc, pb=pb, j0=j0, cw=cw, nkc=nkc:
               e.matmul(sums_ap[0:1, j0 * 128: j0 * 128 + cw], ones_bf1[:, 0:1],
                        ptb[pb][:, j0 * 128: j0 * 128 + cw],
                        start=(kc == 0), stop=(kc == nkc - 1)),
               inc="sT")
            pbuf_user[pb] = cnt("sT")

        def emit_st(kc):
            j0 = max(0, kc - tb_ * 4)
            cw = 512 - j0 * 128
            diag = kc >= tb_ * 4
            bk = stbank[kc % 4]
            wts = ()
            if kc == 0:
                wts = (("sV", nmx_ready[gp]), ("sV", CONSTV),
                       ("sKV", KVTOT), ("sT", VT_T))
            if bk["exp"]:
                wts = wts + (("sA", bk["exp"]),)
            do("tensor", lambda e, kc=kc, j0=j0, cw=cw, H=H, hg_=hg_, qp_=qp_,
               a=bk["ap"]: e.matmul(
                   a[:, j0 * 128: j0 * 128 + cw],
                   k_sb[:, H * S + kc * 128: H * S + (kc + 1) * 128],
                   q_sb[qp_][:, hg_ * 512 + j0 * 128: hg_ * 512 + 512],
                   start=True, stop=False),
               inc=None, waits=wts)
            if diag:
                do("tensor", lambda e, j0=j0, a=bk["ap"]: e.matmul(
                    a[:, j0 * 128:(j0 + 1) * 128], ident[:, :], maskT_sb[:, :],
                    start=False, stop=False), inc=None)
            do("tensor", lambda e, j0=j0, cw=cw, gp=gp, a=bk["ap"]: e.matmul(
                a[:, j0 * 128: j0 * 128 + cw], ones_bf[0:1, 0:128],
                nmxT_sb[gp][0:1, j0 * 128: 512],
                start=False, stop=True),
               inc="sT" if kc % 2 == 1 or kc == nkc - 1 else None)
            return (kc, kc % NPB, j0, cw)

        kc = 0
        pend_exp = []
        while kc < nkc:
            pair = [emit_st(kc)]
            kc += 1
            if kc < nkc:
                pair.append(emit_st(kc))
                kc += 1
            pe_st = cnt("sT")
            for (kci, pb, j0, cw) in pair:
                bk = stbank[kci % 4]
                wex = (("sT", pe_st),) if kci == pair[0][0] else ()
                if pbuf_user[pb]:
                    wex = wex + (("sT", pbuf_user[pb]),)
                bk["exp"] = do("scalar", lambda e, pb=pb, j0=j0, cw=cw, a=bk["ap"]:
                               e.activation(ptb[pb][:, j0 * 128: j0 * 128 + cw],
                                            a[:, j0 * 128: j0 * 128 + cw], AF.Exp,
                                            bias=0.0, scale=1.0),
                               inc="sA", waits=wex)
                pend_pv.append((kci, pb, j0, cw))
            while len(pend_pv) > 2:
                emit_pv(*pend_pv.pop(0))
        while pend_pv:
            emit_pv(*pend_pv.pop(0))
        pe_grp = cnt("sT")
        nmxT_used[gp] = pe_grp
        a1 = do("scalar", lambda e, sp=sp: e.activation(
            recT_sb[sp][0:1, :], sums_ap[0:1, 0:512], AF.Ln, bias=0.0, scale=1.0),
            inc="sA", waits=(("sT", pe_grp),))
        a2 = do("scalar", lambda e, sp=sp: e.mul(
            recT_sb[sp][0:1, :], recT_sb[sp][0:1, :], -1.0), inc="sA",
            waits=(("sA", a1),))
        a3 = do("scalar", lambda e, sp=sp: e.activation(
            recT_sb[sp][0:1, :], recT_sb[sp][0:1, :], AF.Exp, bias=0.0, scale=1.0),
            inc="sA", waits=(("sA", a2),))
        do("tensor", lambda e, sp=sp: e.matmul(
            sums_ap[:, 0:512], ones_f[:, :], recT_sb[sp][0:1, :],
            start=True, stop=True), inc="sT", waits=(("sA", a3),))
        pe_bc = cnt("sT")
        bc = do("scalar", lambda e, sp=sp: CP(e, bcast_sb[sp][:, :],
                                              sums_ap[:, 0:512]),
                inc="sA", waits=(("sT", pe_bc), ("sV", bcast_used[sp])))
        bc_last[g % 2] = bc
        wev = (("sA", bc),)
        if aspill[sp]:
            wev = wev + (("sQA%d" % sp, aspill[sp]),)
        ve = do("vector", lambda e, sp=sp, a=pvb["ap"]: e.tensor_mul(
            astg[sp][:, :], a[:, 0:512], bcast_sb[sp][:, :]),
            inc="sV", waits=wev)
        pvb["free"] = ve
        bcast_used[sp] = ve
        do("sync", lambda e, sp=sp, hg_=hg_, tb_=tb_: e.dma_start(
            out=As[:, hg_ * S + tb_ * 512: hg_ * S + (tb_ + 1) * 512],
            in_=astg[sp][:, :]),
           inc="sQA%d" % sp, n=16, waits=(("sV", ve),))
        aspill[sp] += 16

    qhist_g = {}
    next_nmx = [0]
    for tb in range(TBS):
        qp = tb % 2
        if tb == 0:
            qhist[0] = qload[0]
        if tb + 1 < TBS:
            wts = (("sT", pe_tb_done[tb - 1]),) if tb >= 1 else (
                ("sA", VT_A), ("sV", VT_V), ("sT", VT_T))
            do("scalar", lambda e, tb=tb: e.dma_start(
                out=q_sb[(tb + 1) % 2][:, :],
                in_=Qs[:, (tb + 1) * 16384:(tb + 2) * 16384]),
               inc="sQL%d" % ((tb + 1) % 2), n=16, waits=wts)
            qload[(tb + 1) % 2] += 16
            qhist[tb + 1] = qload[(tb + 1) % 2]
        for hg in range(NQH):
            g = tb * NQH + hg
            while next_nmx[0] < NG and next_nmx[0] <= g + 2:
                gn = next_nmx[0]
                tbn = gn // NQH
                if tbn > tb + 1 or (tbn == tb + 1 and tb + 1 >= TBS):
                    break
                qhist_g[gn] = qhist[tbn]
                mk_nmx(gn)
                next_nmx[0] += 1
            mk_ph2(g)
        pe_tb_done[tb] = cnt("sT")
    wait("sync", ("sQA0", aspill[0]), ("sQA1", aspill[1]))
    kb.barrier()

    sz = NQH * S // 4
    for i in range(4):
        do("scalar", lambda e, i=i, sz=sz: e.dma_start(
            out=arena[:, i * sz:(i + 1) * sz], in_=As[:, i * sz:(i + 1) * sz]),
           inc="sAL", n=16)
    ALTOT = cnt("sAL")
    wfl = [0] * 5
    wuser = [0] * 5
    o2sp = [0, 0]
    for fb in range(8):
        csl = [(fb * 4 + c) % 5 for c in range(4)]
        for c in range(4):
            s5 = csl[c]
            wts = (("sT", wuser[s5]),) if wuser[s5] else ()
            do("gpsimd", lambda e, s5=s5, fb=fb, c=c: e.dma_start(
                out=wslot[s5][:, :], in_=wout[fb, c]),
               inc="sWF%d" % s5, n=16, waits=wts)
            wfl[s5] = cnt("sWF%d" % s5)
        ldw = tuple(("sWF%d" % csl[j], wfl[csl[j]]) for j in range(4))
        for t in range(NT):
            sl = next_slot()
            for c in range(4):
                s5 = csl[c]
                for eci in range(8):
                    ec = c * 8 + eci
                    wts = ()
                    if ec == 0:
                        wts = (("sAL", ALTOT),) + ldw
                        if sl["ev_done"]:
                            wts = wts + (sl["ev_done"],)
                    do("tensor", lambda e, s5=s5, ec=ec, eci=eci, t=t, a=sl["ap"]:
                       e.matmul(a[:, :],
                                arena[:, ec * S + t * 128: ec * S + (t + 1) * 128],
                                wslot[s5][:, eci * 512:(eci + 1) * 512],
                                start=(ec == 0), stop=(ec == NEC - 1)),
                       inc="sT" if ec == NEC - 1 else None, waits=wts)
            sl["pe_done"] = cnt("sT")
            op = (fb * NT + t) % 2
            wev = (("sT", sl["pe_done"]),)
            if o2sp[op]:
                wev = wev + (("sO2%d" % op, o2sp[op]),)
            ev = do(sl["eng"], lambda e, a=sl["ap"], op=op: CP(
                e, o2stg[op][:, :], a[:, :]), inc=ES[sl["eng"]], waits=wev)
            sl["ev_done"] = (ES[sl["eng"]], ev)
            do("sync", lambda e, op=op, fb=fb, t=t: e.dma_start(
                out=out2s[t * 128:(t + 1) * 128, fb * 512:(fb + 1) * 512],
                in_=o2stg[op][:, :]),
               inc="sO2%d" % op, n=16, waits=((ES[sl["eng"]], ev),))
            o2sp[op] += 16
        fbdone = cnt("sT")
        for c in range(4):
            wuser[csl[c]] = fbdone
    wait("sync", ("sO20", o2sp[0]), ("sO21", o2sp[1]))
    kb.barrier()

    x2w = 0
    prev = dict(v=0, a=0, t=0)
    routev = 0
    for t in range(NT):
        lw = ()
        if t >= 1:
            lw = (("sV", prev["v"]), ("sA", prev["a"]), ("sT", prev["t"]),
                  ("sX2W", x2w))
        do("scalar", lambda e, t=t: e.dma_start(
            out=o2_t[:, :], in_=out2s[t * 128:(t + 1) * 128, :]),
           inc="sL5", n=16, waits=lw)
        do("scalar", lambda e, t=t: e.dma_start(
            out=x5_t[:, :], in_=xin[t * 128:(t + 1) * 128, :]), inc="sL5", n=16)
        L5 = cnt("sL5")
        negm, rstd, ar = ln_stats(o2_t[:, :], 0, (("sL5", L5),),
                                  ((("sT", prev["t"]),) if t >= 1 else ()),
                                  dmp=xln3[:, :])
        v1 = do("vector", lambda e, negm=negm, rstd=rstd: e.tensor_scalar(
            ln2tmp[:, :], o2_t[:, :], negm, rstd, op0=ALU.add, op1=ALU.mult),
            inc="sV", waits=(("sA", ar), ("sL5", L5)))
        v2 = do("vector", lambda e: e.tensor_add(ln2tmp[:, :], ln2tmp[:, :], x5_t[:, :]),
                inc="sV", waits=(("sV", v1),))
        do("sync", lambda e, t=t: e.dma_start(
            out=x2s[t * 128:(t + 1) * 128, :], in_=ln2tmp[:, :]),
           inc="sX2W", n=16, waits=(("sV", v2),))
        x2w = cnt("sX2W")
        negm3, rstd3, ar3 = ln_stats(ln2tmp[:, :], 4, (("sV", v2),), dmp=xln3[:, :])
        v3 = do("vector", lambda e, negm3=negm3, rstd3=rstd3: e.tensor_scalar(
            xln3[:, :], ln2tmp[:, :], negm3, rstd3, op0=ALU.add, op1=ALU.mult),
            inc="sV", waits=(("sA", ar3),))
        for g in range(8):
            sl = next_slot()
            for j in range(4):
                fc = g * 4 + j
                wts = ()
                if j == 0:
                    wts = (("sV", v3),)
                    if sl["ev_done"]:
                        wts = wts + (sl["ev_done"],)
                do("tensor", lambda e, fc=fc, j=j, a=sl["ap"]: e.matmul(
                    a[:, j * 128:(j + 1) * 128], xln3[:, fc * 128:(fc + 1) * 128],
                    ident[:, :], start=True, stop=True),
                   inc="sT" if j == 3 else None, waits=wts)
            sl["pe_done"] = cnt("sT")
            evac(sl, arena_dst(g, t))
        aev_pairs = tuple(sl["ev_done"] for sl in SLOTS if sl["ev_done"])
        for ec in range(NEC):
            wts = ()
            if ec == 0:
                wts = aev_pairs + (("sCg", CGTOT),) + ((("sA", routev),) if routev else ())
            do("tensor", lambda e, ec=ec, t=t: e.matmul(
                PE2[:, 0:8], arena[:, ec * S + t * 128: ec * S + (t + 1) * 128],
                gate_sb[:, ec * 8:(ec + 1) * 8],
                start=(ec == 0), stop=(ec == NEC - 1)),
               inc="sT" if ec == NEC - 1 else None, waits=wts)
        pe_rt = cnt("sT")
        vm = do("vector", lambda e: e.reduce_max(
            stats[:, 8:9], PE2[:, 0:8], axis=AX.X, negate=True),
            inc="sV", waits=(("sT", pe_rt),))
        ae = do("scalar", lambda e: e.activation(
            routf[:, :], PE2[:, 0:8], AF.Exp, bias=stats[:, 8:9], scale=1.0,
            accum_out=stats[:, 9:10]), inc="sA", waits=(("sV", vm),))
        vrc = do("vector", lambda e: e.reciprocal(stats[:, 10:11], stats[:, 9:10]),
                 inc="sV", waits=(("sA", ae),))
        vn = do("vector", lambda e: e.tensor_scalar_mul(
            routb[:, :], routf[:, :], stats[:, 10:11]), inc="sV", waits=(("sV", vrc),))
        do("tensor", lambda e: e.matmul(PF[0:8, 0:128], routb[:, :], ident[:, :],
                                        start=True, stop=True),
           inc="sT", waits=(("sV", vn),) + ((("sA", routev),) if routev else ()))
        pe_rt2 = cnt("sT")
        routev = do("scalar", lambda e, t=t: CP(
            e, routT[0:8, t * 128:(t + 1) * 128], PF[0:8, 0:128]),
            inc="sA", waits=(("sT", pe_rt2),))
        prev = dict(v=cnt("sV"), a=routev, t=cnt("sT"))
    kb.barrier()

    for fc in range(NEC):
        for tb in range(TBS):
            sl = next_slot()
            wts = (("sCg", CGTOT),)
            if sl["ev_done"]:
                wts = wts + (sl["ev_done"],)
            do("tensor", lambda e, fc=fc, tb=tb, a=sl["ap"]: e.matmul(
                a[:, :], ics_sb[:, fc * 128:(fc + 1) * 128],
                routT[:, tb * 512:(tb + 1) * 512], start=True, stop=True),
               inc="sT", waits=wts)
            sl["pe_done"] = cnt("sT")
            c_ = do("vector", lambda e, fc=fc, tb=tb, a=sl["ap"]: e.tensor_mul(
                arena[:, fc * S + tb * 512: fc * S + (tb + 1) * 512],
                arena[:, fc * S + tb * 512: fc * S + (tb + 1) * 512], a[:, :]),
                inc="sV", waits=(("sT", sl["pe_done"]),))
            sl["ev_done"] = ("sV", c_)
    XS_V = cnt("sV")
    wfl = [0] * 5
    wuser = [0] * 5
    wsgn = [0] * 5
    ysp = [0, 0]
    os_used = [0, 0]
    osb = [dict(ap=PE2, ev=0), dict(ap=PF, ev=0)]
    for ob in range(8):
        csl = [(ob * 4 + c) % 5 for c in range(4)]
        for c in range(4):
            s5 = csl[c]
            wts = (("sT", wuser[s5]),) if wuser[s5] else ()
            do("gpsimd", lambda e, s5=s5, ob=ob, c=c: e.dma_start(
                out=wslot[s5][:, :], in_=wffn[ob, c]),
               inc="sWF%d" % s5, n=16, waits=wts)
            wfl[s5] = cnt("sWF%d" % s5)
            wsgn[s5] = do("scalar", lambda e, s5=s5: e.activation(
                wslot[s5][:, :], wslot[s5][:, :], AF.Sign, bias=0.0, scale=1.0),
                inc="sA", waits=(("sWF%d" % s5, wfl[s5]),))
        sgw = tuple(("sA", wsgn[csl[j]]) for j in range(4))
        for t in range(NT):
            opar = t % 2
            ob_b = osb[opar]
            wts = ()
            if ob_b["ev"]:
                wts = (("sA", ob_b["ev"]),)
            do("tensor", lambda e, t=t, ob=ob, a=ob_b["ap"]: e.matmul(
                a[:, :], routT[:, t * 128:(t + 1) * 128],
                ocs_sb[:, ob * 512:(ob + 1) * 512], start=True, stop=True),
               inc="sT", waits=wts)
            pe_os = cnt("sT")
            aos = do("scalar", lambda e, opar=opar, a=ob_b["ap"]: CP(
                e, os_sb[opar][:, :], a[:, :]), inc="sA",
                waits=(("sT", pe_os), ("sV", os_used[opar])))
            ob_b["ev"] = aos
            sl = next_slot()
            for c in range(4):
                s5 = csl[c]
                for fci in range(8):
                    fc = c * 8 + fci
                    wts = ()
                    if fc == 0:
                        wts = (("sV", XS_V),) + sgw
                        if sl["ev_done"]:
                            wts = wts + (sl["ev_done"],)
                    do("tensor", lambda e, s5=s5, fc=fc, fci=fci, t=t, a=sl["ap"]:
                       e.matmul(a[:, :],
                                arena[:, fc * S + t * 128: fc * S + (t + 1) * 128],
                                wslot[s5][:, fci * 512:(fci + 1) * 512],
                                start=(fc == 0), stop=(fc == NEC - 1)),
                       inc="sT" if fc == NEC - 1 else None, waits=wts)
            sl["pe_done"] = cnt("sT")
            yp = (ob * NT + t) % 2
            wev = (("sT", sl["pe_done"]), ("sA", aos))
            if ysp[yp]:
                wev = wev + (("sYW%d" % yp, ysp[yp]),)
            ev = do("vector", lambda e, yp=yp, opar=opar, a=sl["ap"]: e.tensor_mul(
                ystg[yp][:, :], a[:, :], os_sb[opar][:, :]), inc="sV", waits=wev)
            os_used[opar] = ev
            sl["ev_done"] = ("sV", ev)
            do("sync", lambda e, yp=yp, ob=ob, t=t: e.dma_start(
                out=ys[t * 128:(t + 1) * 128, ob * 512:(ob + 1) * 512],
                in_=ystg[yp][:, :]),
               inc="sYW%d" % yp, n=16, waits=(("sV", ev),))
            ysp[yp] += 16
        fbdone = cnt("sT")
        for c in range(4):
            wuser[csl[c]] = fbdone
    wait("sync", ("sYW0", ysp[0]), ("sYW1", ysp[1]))
    kb.barrier()

    do("scalar", lambda e: e.dma_start(out=bias_sb[:, :], in_=bias_in[:, :]),
       inc="sL7", n=16)
    bias_ev = 0
    for j in range(8):
        a = PA[:, 0:512] if j % 2 == 0 else PA[:, 512:1024]
        wts = (("sL7", 16),)
        if j >= 2:
            wts = wts + (("sA", bias_ev),)
        do("tensor", lambda e, j=j, a=a: e.matmul(
            a, ones_f[:, :], bias_sb[:, j * 512:(j + 1) * 512], start=True, stop=True),
           inc="sT", waits=wts + (("sV", CONSTV),))
        pe_b = cnt("sT")
        bias_ev = do("scalar", lambda e, j=j, a=a: CP(
            e, bias_bc[:, j * 512:(j + 1) * 512], a), inc="sA", waits=(("sT", pe_b),))
    outw = 0
    pv7 = 0
    for t in range(NT):
        lw = (("sV", pv7),) if t >= 1 else ()
        do("scalar", lambda e, t=t: e.dma_start(
            out=y7_t[:, :], in_=ys[t * 128:(t + 1) * 128, :]),
           inc="sL7", n=16, waits=lw)
        do("scalar", lambda e, t=t: e.dma_start(
            out=x27_t[:, :], in_=x2s[t * 128:(t + 1) * 128, :]), inc="sL7", n=16)
        L7 = cnt("sL7")
        wts = (("sL7", L7), ("sA", bias_ev))
        if outw:
            wts = wts + (("sOUT", outw),)
        v1 = do("vector", lambda e: e.tensor_add(y1_t[:, :], y7_t[:, :], bias_bc[:, :]),
                inc="sV", waits=wts)
        negm, rstd, ar = ln_stats(y1_t[:, :], 0, (("sV", v1),), dmp=y7_t[:, :])
        v2 = do("vector", lambda e, negm=negm, rstd=rstd: e.tensor_scalar(
            y1_t[:, :], y1_t[:, :], negm, rstd, op0=ALU.add, op1=ALU.mult),
            inc="sV", waits=(("sA", ar),))
        v3 = do("vector", lambda e: e.tensor_add(y1_t[:, :], y1_t[:, :], x27_t[:, :]),
                inc="sV", waits=(("sV", v2),))
        pv7 = v3
        do("sync", lambda e, t=t: e.dma_start(
            out=out[t * 128:(t + 1) * 128, :], in_=y1_t[:, :]),
           inc="sOUT", n=16, waits=(("sV", v3),))
        outw = cnt("sOUT")
    wait("sync", ("sOUT", outw))
    kb.barrier()

    with nc.Block() as block:
        block.sync(lambda e: [f(e) for f in kb.ops["sync"]])
        block.scalar(lambda e: [f(e) for f in kb.ops["scalar"]])
        block.vector(lambda e: [f(e) for f in kb.ops["vector"]])
        block.tensor(lambda e: [f(e) for f in kb.ops["tensor"]])
        block.gpsimd(lambda e: [f(e) for f in kb.ops["gpsimd"]])
    return nc


def stage_shared(w_in, w_out, w_ffn, gate_w, ics, ocs, bias, NT=16):
    S = NT * 128
    f32 = np.float32
    wqkv = np.empty((NFC, 128, NEC * 128), f32)
    wt = np.ascontiguousarray(w_in.T)
    for fc in range(NFC):
        r = _frow(fc)
        blk = wt[:, r:r + 128].reshape(NEC, 128, 128)
        wqkv[fc] = blk.transpose(1, 0, 2).reshape(128, NEC * 128)

    def stage_w(w):
        st = np.empty((8, 4, 128, 8 * 512), f32)
        wtt = np.ascontiguousarray(w.T)
        for b in range(8):
            blk = wtt[:, b * 512:(b + 1) * 512].reshape(NEC, 128, 512)
            for c in range(4):
                st[b, c] = blk[c * 8:(c + 1) * 8].transpose(1, 0, 2).reshape(128, 8 * 512)
        return st

    wouts = stage_w(w_out)
    wffns = stage_w(w_ffn)
    gate_stg = np.ascontiguousarray(
        gate_w.T.reshape(NEC, 128, 8).transpose(1, 0, 2).reshape(128, NEC * 8)).astype(f32)
    inv = ROPE_BASE ** (-np.arange(0, D, 2, dtype=f32) / D)
    ang = np.arange(S, dtype=f32)[:, None] * inv[None, :].astype(f32)
    c = np.cos(ang).astype(f32).T
    s = np.sin(ang).astype(f32).T
    alpha = np.float32(1.0 / np.sqrt(np.float32(D)))
    cq = np.concatenate([c, c], 0) * alpha
    sq = np.concatenate([s, s], 0) * alpha
    ck = np.concatenate([c, c], 0)
    sk = np.concatenate([s, s], 0)
    ident = np.eye(128, dtype=BF16NP)
    rot = np.zeros((128, 128), dtype=BF16NP)
    rot[0:64, 64:128] = np.eye(64, dtype=BF16NP)
    rot[64:128, 0:64] = -np.eye(64, dtype=BF16NP)
    mask = np.triu(np.full((128, 128), NEGINF, dtype=f32), k=1).astype(BF16NP)
    maskT = np.tril(np.full((128, 128), NEGINF, dtype=f32), k=-1).astype(BF16NP)
    return dict(
        wqkv=wqkv, wout=wouts, wffn=wffns, gate_st=gate_stg,
        ics_in=np.ascontiguousarray(ics, f32), ocs_in=np.ascontiguousarray(ocs, f32),
        bias_in=np.ascontiguousarray(bias, f32).reshape(1, E),
        ident_in=ident, rot_in=rot, mask_in=mask, maskT_in=maskT,
        cq_in=np.ascontiguousarray(cq), sq_in=np.ascontiguousarray(sq),
        ck_in=np.ascontiguousarray(ck), sk_in=np.ascontiguousarray(sk),
    )


def kernel(L0_attn_x, L0_attn_w_in, L0_attn_w_out, L0_ffn_weight, L0_ffn_bias,
           L0_ffn_gate_w, L0_ffn_in_channel_scale, L0_ffn_out_channel_scale):
    NT = 16
    x = np.ascontiguousarray(np.asarray(L0_attn_x, dtype=np.float32))
    B = x.shape[0]
    shared = stage_shared(np.asarray(L0_attn_w_in, np.float32),
                          np.asarray(L0_attn_w_out, np.float32),
                          np.asarray(L0_ffn_weight, np.float32),
                          np.asarray(L0_ffn_gate_w, np.float32),
                          np.asarray(L0_ffn_in_channel_scale, np.float32),
                          np.asarray(L0_ffn_out_channel_scale, np.float32),
                          np.asarray(L0_ffn_bias, np.float32), NT=NT)
    nc = build_program(NT=NT)
    in_maps = []
    for b in range(B):
        m = dict(shared)
        m["xin"] = np.ascontiguousarray(x[b])
        in_maps.append(m)
    res = run_bass_kernel_spmd(nc, in_maps, core_ids=list(range(B)))
    return np.stack([res.results[b]["out"] for b in range(B)], axis=0)
